# version 05 (final)
# baseline (speedup 1.0000x reference)
import numpy as np
import ml_dtypes
import concourse.bass as bass
import concourse.mybir as mybir
from concourse.bass_utils import run_bass_kernel_spmd
from contextlib import ExitStack

F32 = mybir.dt.float32
BF16 = mybir.dt.bfloat16
I32 = mybir.dt.int32
U8 = mybir.dt.uint8
AF = mybir.ActivationFunctionType
ALU = mybir.AluOpType
AX = mybir.AxisListType

NCORES = 8
T = 4096
D = 1024
NT = T // 128
LN_EPS = 1e-5


class Builder:
    def __init__(self, num_devices=None):
        kw = {"num_devices": num_devices} if num_devices else {}
        self.nc = bass.Bass("TRN2", target_bir_lowering=False,
                            detect_race_conditions=False, **kw)
        self.ctx = ExitStack()

    def inp(self, name, shape, dt):
        return self.nc.dram_tensor(name, shape, dt, kind="ExternalInput").ap()

    def out(self, name, shape, dt):
        return self.nc.dram_tensor(name, shape, dt, kind="ExternalOutput").ap()

    def scratch(self, name, shape, dt):
        return self.nc.dram_tensor(name, shape, dt, kind="Internal").ap()

    def sb(self, name, shape, dt):
        return self.ctx.enter_context(self.nc.sbuf_tensor(name, shape, dt))

    def ps(self, name, shape, dt=F32):
        return self.ctx.enter_context(self.nc.psum_tensor(name, shape, dt))

    def sem(self, name):
        return self.ctx.enter_context(self.nc.semaphore(name))

    def block(self):
        return self.ctx.enter_context(self.nc.Block())

    def done(self):
        self.ctx.close()
        return self.nc


class StageBuilder:

    def __init__(self, root, io, pfx):
        self.root, self.io, self.pfx = root, io, pfx
        self.nc = root.nc
        self.stack = ExitStack()

    def inp(self, name, shape, dt):
        return self.io[name]

    def out(self, name, shape, dt):
        return self.io[name]

    def scratch(self, name, shape, dt):
        return self.nc.dram_tensor(self.pfx + name, shape, dt, kind="Internal").ap()

    def sb(self, name, shape, dt):
        return self.stack.enter_context(
            self.nc.sbuf_tensor(self.pfx + name, shape, dt))

    def ps(self, name, shape, dt=F32):
        return self.stack.enter_context(
            self.nc.psum_tensor(self.pfx + name, shape, dt))

    def sem(self, name):
        return self.root.ctx.enter_context(self.nc.semaphore(self.pfx + name))


def build_launch_D(sb_=None):
    b = sb_ or Builder()
    import os as _os
    HID = 3072
    NH = HID // 128
    TB = 256
    NB = T // TB
    if _os.environ.get("K_NB"):
        NB = int(_os.environ["K_NB"])
    if _os.environ.get("K_RACE"):
        b.nc._state().__class__
    x3_d = b.inp("x3", [T, D], F32)
    x3T_d = b.inp("x3T", [D, T], F32)
    wgT_d = b.inp("wgT", [D, HID], I32)
    wuT_d = b.inp("wuT", [D, HID], I32)
    wdT_d = b.inp("wdT", [HID, D], I32)
    x4_d = b.out("x4", [T, D], F32)

    wg_bf = b.sb("wg_bf", [128, 8 * HID], BF16)
    wu_bf = b.sb("wu_bf", [128, 8 * HID], BF16)
    wd_bf = b.sb("wd_bf", [128, NH * D], BF16)
    wtmp = b.sb("wtmp", [128, HID], I32)
    xT_f = b.sb("xT_f", [128, 8 * TB], F32)
    xT_b = b.sb("xT_b", [128, 8 * TB], BF16)
    x3_f = b.sb("x3_f", [128, 2 * D], F32)
    sg_b = b.sb("sg_b", [128, TB], BF16)
    up_b = b.sb("up_b", [128, TB], BF16)
    h_bf = b.sb("h_bf", [128, NH * TB], BF16)
    o_f = b.sb("o_f", [128, 2 * D], F32)
    x4_s = b.sb("x4_s", [128, 2 * D], F32)
    lnc = b.sb("lnc", [128, 8], F32)
    cpool = b.sb("cpool", [128, 4], F32)

    ps_g = b.ps("ps_g", [128, TB])
    ps_u = b.ps("ps_u", [128, TB])
    ps_d = [b.ps(f"ps_d{i}", [128, 512]) for i in range(4)]
    ps_x = b.ps("ps_x", [128, 128])

    ld = b.sem("ld")
    wcv = b.sem("wcv")
    xcv = b.sem("xcv")
    mmg = b.sem("mmg")
    evg = b.sem("evg")
    mmd = b.sem("mmd")
    evd = b.sem("evd")
    st0 = b.sem("st0")
    st1 = b.sem("st1")
    vse = b.sem("vse")

    LD_W = 16 * 40
    def LD_B(bi):
        return LD_W + 160 * (bi + 1)

    def EV(bi, tc, k):
        return 10 * bi + 5 * tc + k


    def _fn_sync(sync):
        for ec in range(8):
            if ec >= 1:
                sync.wait_ge(wcv, ec)
            sync.dma_start(out=wtmp[:, :],
                           in_=wgT_d[ec * 128:(ec + 1) * 128, :]).then_inc(ld, 16)
        for ec in range(8):
            sync.wait_ge(wcv, 8 + ec)
            sync.dma_start(out=wtmp[:, :],
                           in_=wuT_d[ec * 128:(ec + 1) * 128, :]).then_inc(ld, 16)
        for hc in range(NH):
            sync.wait_ge(wcv, 16 + hc)
            sync.dma_start(out=wtmp[:, 0:D],
                           in_=wdT_d[hc * 128:(hc + 1) * 128, :]).then_inc(ld, 16)
        for bi in range(NB):
            if bi == 0:
                sync.wait_ge(wcv, 40)
            if bi >= 1:
                sync.wait_ge(evd, 10 * bi)
                sync.wait_ge(xcv, bi)
            for ec in range(8):
                sync.dma_start(out=xT_f[:, ec * TB:(ec + 1) * TB],
                               in_=x3T_d[ec * 128:(ec + 1) * 128, bi * TB:(bi + 1) * TB]
                               ).then_inc(ld, 16)
            for tc in range(2):
                sync.dma_start(out=x3_f[:, tc * D:(tc + 1) * D],
                               in_=x3_d[bi * TB + tc * 128: bi * TB + (tc + 1) * 128, :]
                               ).then_inc(ld, 16)

    def _fn_vector(vector):
        nse = [0]
        def edge(ins):
            nse[0] += 1
            ins.then_inc(vse, 1)
            vector.wait_ge(vse, nse[0])
        vector.memset(cpool[:, 0:1], LN_EPS)
        for ec in range(8):
            vector.wait_ge(ld, 16 * (ec + 1))
            vector.tensor_copy(wg_bf[:, ec * HID:(ec + 1) * HID],
                               wtmp[:, :]).then_inc(wcv, 1)
        for ec in range(8):
            vector.wait_ge(ld, 16 * (8 + ec + 1))
            vector.tensor_copy(wu_bf[:, ec * HID:(ec + 1) * HID],
                               wtmp[:, :]).then_inc(wcv, 1)
        for hc in range(NH):
            vector.wait_ge(ld, 16 * (16 + hc + 1))
            vector.tensor_copy(wd_bf[:, hc * D:(hc + 1) * D],
                               wtmp[:, 0:D]).then_inc(wcv, 1)
        for bi in range(NB):
            vector.wait_ge(ld, LD_B(bi))
            if bi >= 1:
                vector.wait_ge(mmg, 2 * NH * bi)
            vector.tensor_scalar(xT_b[:], xT_f[:], 1.0 / 127.0, None, ALU.mult
                                 ).then_inc(xcv, 1)
            for hc in range(NH):
                k = bi * NH + hc
                vector.wait_ge(mmg, 2 * k + 2)
                vector.tensor_copy(up_b[:], ps_u[:]).then_inc(evg, 1)
                vector.wait_ge(evg, 3 * k + 2)
                vector.tensor_tensor(h_bf[:, hc * TB:(hc + 1) * TB], sg_b[:], up_b[:],
                                     ALU.mult).then_inc(evg, 1)
            for tc in range(2):
                for eh in range(2):
                    vector.wait_ge(mmd, 4 * bi + 2 * tc + eh + 1)
                    vector.scalar_tensor_tensor(
                        o_f[:, tc * D + eh * 512: tc * D + eh * 512 + 512],
                        ps_d[2 * tc + eh][:], 1.0 / 127.0,
                        x3_f[:, tc * D + eh * 512: tc * D + eh * 512 + 512],
                        ALU.mult, ALU.add)
                o = o_f[:, tc * D:(tc + 1) * D]
                vector.reduce_sum(lnc[:, 0:1], o, AX.X).then_inc(evd, 1)
                vector.wait_ge(evd, EV(bi, tc, 2))
                vector.tensor_scalar(lnc[:, 2:3], lnc[:, 0:1], 1.0 / D, None, ALU.mult)
                vector.tensor_scalar(lnc[:, 3:4], lnc[:, 1:2], 1.0 / D, None, ALU.mult)
                edge(vector.tensor_tensor(lnc[:, 4:5], lnc[:, 2:3], lnc[:, 2:3],
                                          ALU.mult))
                vector.tensor_sub(lnc[:, 5:6], lnc[:, 3:4], lnc[:, 4:5]
                                  ).then_inc(evd, 1)
                vector.wait_ge(evd, EV(bi, tc, 4))
                edge(vector.reciprocal(lnc[:, 7:8], lnc[:, 6:7]))
                if bi >= 1:
                    vector.wait_ge(st0 if tc == 0 else st1, 16 * bi)
                vector.tensor_scalar(x4_s[:, tc * D:(tc + 1) * D], o,
                                     lnc[:, 2:3], lnc[:, 7:8],
                                     ALU.subtract, ALU.mult).then_inc(evd, 1)

    def _fn_scalar(scalar):
        for bi in range(NB):
            for hc in range(NH):
                k = bi * NH + hc
                scalar.wait_ge(mmg, 2 * k + 1)
                if k >= 1:
                    scalar.wait_ge(evg, 3 * k)
                scalar.activation(sg_b[:], ps_g[:], AF.Silu).then_inc(evg, 1)
            for tc in range(2):
                scalar.wait_ge(evd, EV(bi, tc, 1))
                o = o_f[:, tc * D:(tc + 1) * D]
                scalar.activation(xT_f[:, 0:D], o, AF.Square, accum_out=lnc[:, 1:2]
                                  ).then_inc(evd, 1)
                scalar.wait_ge(evd, EV(bi, tc, 3))
                scalar.activation(lnc[:, 6:7], lnc[:, 5:6], AF.Sqrt, bias=cpool[:, 0:1]
                                  ).then_inc(evd, 1)

    def _fn_tensor(tensor):
        for bi in range(NB):
            tensor.wait_ge(xcv, bi + 1)
            for hc in range(NH):
                k = bi * NH + hc
                if k >= 1:
                    tensor.wait_ge(evg, 3 * k)
                for ec in range(8):
                    if k == 0:
                        tensor.wait_ge(wcv, ec + 1)
                    tensor.matmul(ps_g[:, :],
                                  wg_bf[:, ec * HID + hc * 128: ec * HID + hc * 128 + 128],
                                  xT_b[:, ec * TB:(ec + 1) * TB],
                                  start=(ec == 0), stop=(ec == 7))
                tensor.matmul(ps_x[:, :], wg_bf[:, 0:128], xT_b[:, 0:128],
                              start=True, stop=True).then_inc(mmg, 1)
                for ec in range(8):
                    if k == 0:
                        tensor.wait_ge(wcv, 8 + ec + 1)
                    tensor.matmul(ps_u[:, :],
                                  wu_bf[:, ec * HID + hc * 128: ec * HID + hc * 128 + 128],
                                  xT_b[:, ec * TB:(ec + 1) * TB],
                                  start=(ec == 0), stop=(ec == 7))
                tensor.matmul(ps_x[:, :], wg_bf[:, 0:128], xT_b[:, 0:128],
                              start=True, stop=True).then_inc(mmg, 1)
            tensor.wait_ge(evg, 3 * NH * (bi + 1))
            if bi == 0:
                tensor.wait_ge(wcv, 40)
            for tc in range(2):
                for eh in range(2):
                    if bi >= 1:
                        tensor.wait_ge(evd, EV(bi - 1, tc, 1))
                    for hc in range(NH):
                        tensor.matmul(ps_d[2 * tc + eh][:, :],
                                      h_bf[:, hc * TB + tc * 128: hc * TB + tc * 128 + 128],
                                      wd_bf[:, hc * D + eh * 512: hc * D + eh * 512 + 512],
                                      start=(hc == 0), stop=(hc == NH - 1))
                    tensor.matmul(ps_x[:, :], wg_bf[:, 0:128], xT_b[:, 0:128],
                                  start=True, stop=True).then_inc(mmd, 1)

    def _fn_gpsimd(gpsimd):
        for bi in range(NB):
            for tc in range(2):
                gpsimd.wait_ge(evd, EV(bi, tc, 5))
                gpsimd.dma_start(out=x4_d[bi * TB + tc * 128: bi * TB + (tc + 1) * 128, :],
                                 in_=x4_s[:, tc * D:(tc + 1) * D]
                                 ).then_inc(st0 if tc == 0 else st1, 16)
        gpsimd.wait_ge(st0, 16 * NB)
        gpsimd.wait_ge(st1, 16 * NB)

    fns = {'sy': _fn_sync, 'v': _fn_vector, 's': _fn_scalar,
           't': _fn_tensor, 'g': _fn_gpsimd, 'ps_x': ps_x, 'idn': None}
    if sb_ is not None:
        return fns
    blk = b.block()

    @blk.sync
    def _(sync):
        fns['sy'](sync)

    @blk.vector
    def _(vector):
        fns['v'](vector)

    @blk.scalar
    def _(scalar):
        fns['s'](scalar)

    @blk.tensor
    def _(tensor):
        fns['t'](tensor)

    @blk.gpsimd
    def _(gpsimd):
        fns['g'](gpsimd)

    return b.done()


def run_launch_D(x3_full, wg, wu, wd):
    nc = build_launch_D()
    x3 = np.ascontiguousarray(x3_full.reshape(NCORES * T, D))
    wgT = np.ascontiguousarray(wg.T)
    wuT = np.ascontiguousarray(wu.T)
    wdT = np.ascontiguousarray(wd.T)
    in_maps = []
    for c in range(NCORES):
        xs = np.ascontiguousarray(x3[c * T:(c + 1) * T])
        in_maps.append({"x3": xs, "x3T": np.ascontiguousarray(xs.T),
                        "wgT": wgT, "wuT": wuT, "wdT": wdT})
    res = run_bass_kernel_spmd(nc, in_maps, core_ids=list(range(NCORES)))
    x4 = np.concatenate([r["x4"] for r in res.results], 0)
    return x4.reshape(4, 8192, D)


def kernel(**inputs):
    import os as _os
    if not _os.environ.get("K_UNFUSED"):
        return kernel_fused(inputs)
    x1 = run_launch_A(inputs["L0_attn_x"], inputs["L0_attn_Wq"],
                      inputs["L0_attn_Wk"], inputs["L0_attn_Wv"])
    x2 = run_launch_B(x1, inputs["L0_ffn_w_gate"], inputs["L0_ffn_w_up"],
                      inputs["L0_ffn_w_down"])
    c1res = run_launch_C1(x2, inputs["L1_attn_W_in"], inputs["L1_attn_b_in"],
                          inputs["L1_attn_W_qkva"], inputs["L1_attn_b_qkva"])
    x3 = run_launch_C2(x2, c1res, inputs["L1_attn_W_y"], inputs["L1_attn_b_y"],
                       inputs["L1_attn_W_out"], inputs["L1_attn_b_out"],
                       inputs["L1_attn_h0_re"], inputs["L1_attn_h0_im"])
    x4 = run_launch_D(x3, inputs["L1_ffn_w_gate"], inputs["L1_ffn_w_up"],
                      inputs["L1_ffn_w_down"])
    return np.ascontiguousarray(np.asarray(x4, np.float32))



M23 = float(2 ** 23)


class QuantV:

    def __init__(self, vector, qs, qv, vse, qa, qb, qc, qt, qbm, qsc, qrs):
        self.v = vector
        self.qs, self.qv, self.vse = qs, qv, vse
        self.i = 0
        self.nse = [0]
        self.qa, self.qb, self.qc, self.qt = qa, qb, qc, qt
        self.qbm, self.qsc, self.qrs = qbm, qsc, qrs

    def _edge(self, ins):
        self.nse[0] += 1
        ins.then_inc(self.vse, 1)
        self.v.wait_ge(self.vse, self.nse[0])

    def prologue(self, X, W):
        v = self.v
        nb = W // 16
        X3 = X.rearrange("p (b k) -> p b k", k=16)
        self._edge(v.tensor_reduce(self.qbm[:, 0:nb], X3, AX.X, ALU.max,
                                   apply_absolute_value=True))
        self._edge(v.tensor_scalar_mul(self.qsc[:, 0:nb], self.qbm[:, 0:nb], 1.0 / 6.0))
        self._edge(v.tensor_scalar_max(self.qsc[:, 0:nb], self.qsc[:, 0:nb], 1e-12))
        self._edge(v.reciprocal(self.qrs[:, 0:nb], self.qsc[:, 0:nb]))

    def emit(self, X, Q, F, boff):
        v = self.v
        i = self.i
        nb = F // 16
        A, Bu, C = self.qa[:, 0:F], self.qb[:, 0:F], self.qc[:, 0:F]
        v.wait_ge(self.qs, 2 * i + 1)
        A3 = A.rearrange("p (b k) -> p b k", k=16)
        rs3 = self.qrs[:, boff:boff + nb].unsqueeze(2).broadcast_to([128, nb, 16])
        v.tensor_tensor(A3, A3, rs3, ALU.mult)
        v.tensor_scalar_min(Bu, A, 2.0).then_inc(self.qv, 1)
        v.wait_ge(self.qs, 2 * i + 2)
        v.tensor_single_scalar(self.qt[:, 0:F], A, 2.5, ALU.is_gt)
        v.tensor_add(Bu, Bu, self.qt[:, 0:F])
        v.tensor_single_scalar(self.qt[:, 0:F], A, 3.5, ALU.is_gt)
        v.tensor_add(Bu, Bu, self.qt[:, 0:F])
        v.tensor_single_scalar(self.qt[:, 0:F], A, 5.0, ALU.is_gt)
        v.scalar_tensor_tensor(Bu, self.qt[:, 0:F], 2.0, Bu, ALU.mult, ALU.add)
        sc3 = self.qsc[:, boff:boff + nb].unsqueeze(2).broadcast_to([128, nb, 16])
        C3 = C.rearrange("p (b k) -> p b k", k=16)
        self.last_c3 = v.tensor_tensor(C3, C3, sc3, ALU.mult)
        ins = v.tensor_tensor(Q, Bu, C, ALU.mult)
        ins.then_inc(self.qv, 1)
        self.i += 1
        return ins


class QuantS:

    def __init__(self, scalar, qs, qv, qa, qb, qc, qt, cpool):
        self.s = scalar
        self.qs, self.qv = qs, qv
        self.i = 0
        self.qa, self.qb, self.qc, self.qt = qa, qb, qc, qt
        self.cp = cpool

    def emit(self, X, F):
        s = self.s
        i = self.i
        A, Bu, C = self.qa[:, 0:F], self.qb[:, 0:F], self.qc[:, 0:F]
        s.wait_ge(self.qv, 2 * i)
        s.activation(A, X, AF.Abs)
        s.activation(C, X, AF.Sign).then_inc(self.qs, 1)
        s.wait_ge(self.qv, 2 * i + 1)
        s.activation(self.qt[:, 0:F], Bu, AF.Relu, bias=self.cp[:, 1:2],
                     scale=self.cp[:, 2:3])
        s.activation(Bu, self.qt[:, 0:F], AF.Relu, bias=self.cp[:, 3:4],
                     scale=self.cp[:, 4:5]).then_inc(self.qs, 1)
        self.i += 1


def build_launch_B(sb_=None):
    b = sb_ or Builder()
    import os as _os
    HID = 2816
    NH = HID // 128
    HS = [(0, 512), (512, 512), (1024, 512), (1536, 512), (2048, 512), (2560, 256)]
    TB = 256
    NB = T // TB
    if _os.environ.get("K_NB"):
        NB = int(_os.environ["K_NB"])
    x1_d = b.inp("x1", [T, D], F32)
    wg_d = b.inp("wg", [HID, D], F32)
    wu_d = b.inp("wu", [HID, D], F32)
    wd_d = b.inp("wd", [D, HID], F32)
    idn_d = b.inp("idn", [128, 128], BF16)
    x2_d = b.out("x2", [T, D], F32)
    if _os.environ.get("K_DBG"):
        dbg_xq_d = b.out("dbg_xq", [128, 2 * D], BF16)
        dbg_h_d = b.out("dbg_h", [128, 2 * HID], BF16)
        dbg_hq_d = b.out("dbg_hq", [128, 2 * HID], BF16)
        dbg_xqT_d = b.out("dbg_xqT", [128, 8 * TB], BF16)
        dbg_hqT_d = b.out("dbg_hqT", [128, NH * TB], BF16)
        dbg_wg_d = b.out("dbg_wg", [128, 8 * HID], BF16)
        dbg_wu_d = b.out("dbg_wu", [128, 8 * HID], BF16)
        dbg_ps_d = b.out("dbg_ps", [128, 1024], F32)
        dbg_sg_d = b.out("dbg_sg", [128, 1024], F32)

    wgqT = b.sb("wgqT", [128, 8 * HID], BF16)
    wuqT = b.sb("wuqT", [128, 8 * HID], BF16)
    wdqT = b.sb("wdqT", [128, NH * D], BF16)
    h_b = b.sb("h_b", [128, 2 * HID], BF16)
    hq = b.sb("hq", [128, 2 * HID], BF16)
    hqT = b.sb("hqT", [128, NH * TB], BF16)
    xf = b.sb("xf", [128, 2 * D], F32)
    xq = b.sb("xq", [128, 2 * D], BF16)
    xqT = b.sb("xqT", [128, 8 * TB], BF16)
    sg_f = b.sb("sg_f", [128, 512], F32)
    up_f = b.sb("up_f", [128, 512], F32)
    qa = b.sb("qa", [128, 512], F32)
    qb = b.sb("qb", [128, 512], F32)
    qc = b.sb("qc", [128, 512], F32)
    qt = b.sb("qt", [128, 512], F32)
    qbm = b.sb("qbm", [128, 176], F32)
    qsc = b.sb("qsc", [128, 176], F32)
    qrs = b.sb("qrs", [128, 176], F32)
    o_f = b.sb("o_f", [128, 2 * D], F32)
    lns = b.sb("lns", [128, D], F32)
    lnc = b.sb("lnc", [128, 8], F32)
    cpool = b.sb("cpool", [128, 6], F32)
    idn = b.sb("idn_s", [128, 128], BF16)

    ps_t_full = b.ps("ps_t", [128, 1024], BF16)
    ps_t2_full = b.ps("ps_t2", [128, 1024], BF16)
    ps_t = ps_t_full[:, 0:128]
    ps_t2 = ps_t2_full[:, 0:128]
    ps_a = b.ps("ps_a", [128, 512])
    ps_b = b.ps("ps_b", [128, 512])
    ps_d0 = b.ps("ps_d0", [128, 512])
    ps_d1 = b.ps("ps_d1", [128, 512])
    ps_x = b.ps("ps_x", [128, 128])

    ld = b.sem("ld")
    qs = b.sem("qs")
    qv = b.sem("qv")
    wqc = b.sem("wqc")
    ttt = b.sem("ttt")
    tte = b.sem("tte")
    mmgu = b.sem("mmgu")
    evgu = b.sem("evgu")
    mmd = b.sem("mmd")
    ev = b.sem("ev")
    st0 = b.sem("st0")
    st1 = b.sem("st1")
    vse = b.sem("vse")

    NW = 2 * NH + 8
    WW = [D] * (2 * NH) + [HID] * 8
    TRN = [8] * (2 * NH) + [NH] * 8
    TRB = [0]
    for t_ in TRN:
        TRB.append(TRB[-1] + t_)
    TRW = TRB[-1]
    TRPB = 16 + 2 * NH

    def slices(w):
        out = []
        off = 0
        while off < w:
            f = min(512, w - off)
            out.append((off, f))
            off += f
        return out

    def quant_calls_per_tile(w):
        return len(slices(w))

    QV_PER_SLICE = 2
    QS_PER_SLICE = 2
    qv_after_wtile = []
    tot = 0
    for wi in range(NW):
        tot += QV_PER_SLICE * quant_calls_per_tile(WW[wi])
        qv_after_wtile.append(tot)
    QV_W = tot
    QV_XB = QV_PER_SLICE * 4
    QV_HB = QV_PER_SLICE * 12

    def qv_after_xquant(bi):
        return QV_W + (QV_XB + QV_HB) * bi + QV_XB

    def qv_after_hquant(bi, tt):
        return QV_W + (QV_XB + QV_HB) * bi + QV_XB + QV_PER_SLICE * 6 * (tt + 1)

    def EVA(bi, tc, k):
        return 10 * bi + 5 * tc + k


    def _fn_sync(sync):
        sync.dma_start(out=idn[:, :], in_=idn_d[:, :]).then_inc(ld, 16)
        h_f32 = h_b[:].bitcast(F32)
        for wi in range(NW):
            w = WW[wi]
            if w == HID and wi >= 1:
                sync.wait_ge(wqc, wi)
            elif wi >= 2:
                sync.wait_ge(wqc, wi - 1)
            sl = (wi % 2)
            if wi < NH:
                srcd = wg_d[wi * 128:(wi + 1) * 128, :]
            elif wi < 2 * NH:
                srcd = wu_d[(wi - NH) * 128:(wi - NH + 1) * 128, :]
            else:
                srcd = wd_d[(wi - 2 * NH) * 128:(wi - 2 * NH + 1) * 128, :]
            if w == D:
                dst = h_f32[:, sl * D: sl * D + D]
            else:
                dst = h_f32[:, 0:HID]
            sync.dma_start(out=dst, in_=srcd).then_inc(ld, 16)
        for bi in range(NB):
            if bi == 0:
                sync.wait_ge(wqc, NW)
            else:
                sync.wait_ge(ev, 10 * bi)
            for tc in range(2):
                sync.dma_start(out=xf[:, tc * D:(tc + 1) * D],
                               in_=x1_d[bi * TB + tc * 128: bi * TB + (tc + 1) * 128, :]
                               ).then_inc(ld, 16)

    def LDW(wi):
        return 16 * (1 + wi + 1)

    def LDB(bi):
        return 16 * (1 + NW) + 32 * (bi + 1)

    def _fn_vector(vector):
        vector.memset(cpool[:, 0:1], LN_EPS)
        vector.memset(cpool[:, 1:2], M23)
        vector.memset(cpool[:, 2:3], 2.0)
        vector.memset(cpool[:, 3:4], -float(2 ** 22))
        vector.memset(cpool[:, 4:5], 0.5)
        qctx = QuantV(vector, qs, qv, vse, qa, qb, qc, qt, qbm, qsc, qrs)
        h_f32 = h_b[:].bitcast(F32)
        gtr = 0
        for wi in range(NW):
            w = WW[wi]
            sl = (wi % 2)
            vector.wait_ge(ld, LDW(wi))
            if wi >= 1:
                vector.wait_ge(ttt, TRB[wi])
            src_all = h_f32[:, sl * D: sl * D + w] if w == D else h_f32[:, 0:HID]
            qctx.prologue(src_all, w)
            for (off, f) in slices(w):
                qctx.emit(src_all[:, off:off + f], hq[:, off:off + f], f, off // 16)
            qctx.last_c3.then_inc(wqc, 1)
            for tr in range(TRN[wi]):
                vector.wait_ge(ttt, gtr + 1)
                pst = ps_t if gtr % 2 == 0 else ps_t2
                if wi < NH:
                    dstw = wgqT[:, tr * HID + wi * 128: tr * HID + wi * 128 + 128]
                elif wi < 2 * NH:
                    t2 = wi - NH
                    dstw = wuqT[:, tr * HID + t2 * 128: tr * HID + t2 * 128 + 128]
                else:
                    ec = wi - 2 * NH
                    dstw = wdqT[:, tr * D + ec * 128: tr * D + ec * 128 + 128]
                vector.tensor_copy(dstw, pst[:, :]).then_inc(tte, 1)
                gtr += 1
        for bi in range(NB):
            vector.wait_ge(ld, LDB(bi))
            if bi >= 1:
                vector.wait_ge(mmgu, 24 * bi)
            for tc in range(2):
                qctx.prologue(xf[:, tc * D:(tc + 1) * D], D)
                for (off, f) in slices(D):
                    qctx.emit(xf[:, tc * D + off: tc * D + off + f],
                              xq[:, tc * D + off: tc * D + off + f], f, off // 16)
            for g in range(16):
                vector.wait_ge(ttt, gtr + 1)
                pst = ps_t if gtr % 2 == 0 else ps_t2
                tc, ec = g // 8, g % 8
                vector.tensor_copy(xqT[:, ec * TB + tc * 128: ec * TB + tc * 128 + 128],
                                   pst[:, :]).then_inc(tte, 1)
                gtr += 1
            for tt in range(2):
                for si, (off, f) in enumerate(HS):
                    K2 = bi * 12 + tt * 6 + si
                    vector.wait_ge(mmgu, 2 * K2 + 2)
                    vector.tensor_copy(up_f[:, 0:f], ps_b[:, 0:f]).then_inc(evgu, 1)
                    vector.wait_ge(evgu, 3 * K2 + 2)
                    vector.tensor_tensor(h_b[:, tt * HID + off: tt * HID + off + f],
                                         sg_f[:, 0:f], up_f[:, 0:f], ALU.mult
                                         ).then_inc(evgu, 1)
                qctx.prologue(h_b[:, tt * HID:(tt + 1) * HID], HID)
                for (off, f) in slices(HID):
                    qctx.emit(h_b[:, tt * HID + off: tt * HID + off + f],
                              hq[:, tt * HID + off: tt * HID + off + f], f, off // 16)
                for hc in range(NH):
                    vector.wait_ge(ttt, gtr + 1)
                    pst = ps_t if gtr % 2 == 0 else ps_t2
                    vector.tensor_copy(
                        hqT[:, hc * TB + tt * 128: hc * TB + tt * 128 + 128],
                        pst[:, :]).then_inc(tte, 1)
                    gtr += 1
            for tc in range(2):
                if bi >= 1:
                    vector.wait_ge(st0 if tc == 0 else st1, 16 * bi)
                for eh in range(2):
                    vector.wait_ge(mmd, 4 * bi + 2 * tc + eh + 1)
                    vector.scalar_tensor_tensor(
                        o_f[:, tc * D + eh * 512: tc * D + eh * 512 + 512],
                        (ps_d0 if eh == 0 else ps_d1)[:, :], 1.0,
                        xf[:, tc * D + eh * 512: tc * D + eh * 512 + 512],
                        ALU.mult, ALU.add)
                o = o_f[:, tc * D:(tc + 1) * D]
                vector.reduce_sum(lnc[:, 0:1], o, AX.X).then_inc(ev, 1)
                vector.wait_ge(ev, EVA(bi, tc, 2))
                vector.tensor_scalar(lnc[:, 2:3], lnc[:, 0:1], 1.0 / D, None, ALU.mult)
                vector.tensor_scalar(lnc[:, 3:4], lnc[:, 1:2], 1.0 / D, None, ALU.mult)
                qctx._edge(vector.tensor_tensor(lnc[:, 4:5], lnc[:, 2:3], lnc[:, 2:3],
                                                ALU.mult))
                vector.tensor_sub(lnc[:, 5:6], lnc[:, 3:4], lnc[:, 4:5]
                                  ).then_inc(ev, 1)
                vector.wait_ge(ev, EVA(bi, tc, 4))
                qctx._edge(vector.reciprocal(lnc[:, 7:8], lnc[:, 6:7]))
                vector.tensor_scalar(o, o,
                                     lnc[:, 2:3], lnc[:, 7:8],
                                     ALU.subtract, ALU.mult).then_inc(ev, 1)
        if _os.environ.get("K_DBG") and NB == 1:
            vector.tensor_copy(lns[:, 0:512], ps_a[:, :])
            vector.tensor_copy(lns[:, 512:1024], ps_b[:, :])
            vector.tensor_copy(o_f[:, 0:512], sg_f[:, :])
            vector.tensor_copy(o_f[:, 512:1024], up_f[:, :]).then_inc(ev, 1)

    def _fn_scalar(scalar):
        sctx = QuantS(scalar, qs, qv, qa, qb, qc, qt, cpool)
        h_f32 = h_b[:].bitcast(F32)
        for wi in range(NW):
            w = WW[wi]
            sl = (wi % 2)
            scalar.wait_ge(ld, LDW(wi))
            src_all = h_f32[:, sl * D: sl * D + w] if w == D else h_f32[:, 0:HID]
            for (off, f) in slices(w):
                sctx.emit(src_all[:, off:off + f], f)
        for bi in range(NB):
            scalar.wait_ge(ld, LDB(bi))
            for tc in range(2):
                for (off, f) in slices(D):
                    sctx.emit(xf[:, tc * D + off: tc * D + off + f], f)
            for tt in range(2):
                for si, (off, f) in enumerate(HS):
                    K2 = bi * 12 + tt * 6 + si
                    scalar.wait_ge(mmgu, 2 * K2 + 1)
                    if K2 >= 1:
                        scalar.wait_ge(evgu, 3 * K2)
                    scalar.activation(sg_f[:, 0:f], ps_a[:, 0:f], AF.Silu
                                      ).then_inc(evgu, 1)
                for (off, f) in slices(HID):
                    sctx.emit(h_b[:, tt * HID + off: tt * HID + off + f], f)
            for tc in range(2):
                scalar.wait_ge(ev, EVA(bi, tc, 1))
                o = o_f[:, tc * D:(tc + 1) * D]
                scalar.activation(lns[:], o, AF.Square, accum_out=lnc[:, 1:2]
                                  ).then_inc(ev, 1)
                scalar.wait_ge(ev, EVA(bi, tc, 3))
                scalar.activation(lnc[:, 6:7], lnc[:, 5:6], AF.Sqrt, bias=cpool[:, 0:1]
                                  ).then_inc(ev, 1)

    def _fn_tensor(tensor):
        gtr = 0
        for wi in range(NW):
            for tr in range(TRN[wi]):
                tensor.wait_ge(qv, qv_after_wtile[wi])
                if gtr >= 2:
                    tensor.wait_ge(tte, gtr - 1)
                pst = ps_t if gtr % 2 == 0 else ps_t2
                tensor.transpose(pst[:, :], hq[:, tr * 128: tr * 128 + 128], idn[:, :])
                tensor.matmul(ps_x[:, :], idn[:, :], idn[:, :],
                              start=True, stop=True).then_inc(ttt, 1)
                gtr += 1
        for bi in range(NB):
            for g in range(16):
                tensor.wait_ge(qv, qv_after_xquant(bi))
                if gtr >= 2:
                    tensor.wait_ge(tte, gtr - 1)
                pst = ps_t if gtr % 2 == 0 else ps_t2
                tc, ec = g // 8, g % 8
                tensor.transpose(pst[:, :], xq[:, tc * D + ec * 128: tc * D + ec * 128 + 128],
                                 idn[:, :])
                tensor.matmul(ps_x[:, :], idn[:, :], idn[:, :],
                              start=True, stop=True).then_inc(ttt, 1)
                gtr += 1
            for tt in range(2):
                for si, (off, f) in enumerate(HS):
                    K2 = bi * 12 + tt * 6 + si
                    if K2 >= 1:
                        tensor.wait_ge(evgu, 3 * K2)
                    tensor.wait_ge(tte, TRW + TRPB * bi + 16)
                    for ec in range(8):
                        tensor.matmul(ps_a[:, 0:f],
                                      xqT[:, ec * TB + tt * 128: ec * TB + tt * 128 + 128],
                                      wgqT[:, ec * HID + off: ec * HID + off + f],
                                      start=(ec == 0), stop=(ec == 7))
                    tensor.matmul(ps_x[:, :], idn[:, :], idn[:, :],
                                  start=True, stop=True).then_inc(mmgu, 1)
                    for ec in range(8):
                        tensor.matmul(ps_b[:, 0:f],
                                      xqT[:, ec * TB + tt * 128: ec * TB + tt * 128 + 128],
                                      wuqT[:, ec * HID + off: ec * HID + off + f],
                                      start=(ec == 0), stop=(ec == 7))
                    tensor.matmul(ps_x[:, :], idn[:, :], idn[:, :],
                                  start=True, stop=True).then_inc(mmgu, 1)
                for hc in range(NH):
                    tensor.wait_ge(qv, qv_after_hquant(bi, tt))
                    if gtr >= 2:
                        tensor.wait_ge(tte, gtr - 1)
                    pst = ps_t if gtr % 2 == 0 else ps_t2
                    tensor.transpose(pst[:, :],
                                     hq[:, tt * HID + hc * 128: tt * HID + hc * 128 + 128],
                                     idn[:, :])
                    tensor.matmul(ps_x[:, :], idn[:, :], idn[:, :],
                                  start=True, stop=True).then_inc(ttt, 1)
                    gtr += 1
            for tc in range(2):
                for eh in range(2):
                    tensor.wait_ge(tte, TRW + TRPB * bi + 16 + NH * (tc + 1))
                    if bi >= 1 or tc >= 1:
                        pb, ptc = (bi, 1) if tc == 0 and bi >= 1 else (bi, 0)
                        if tc == 0:
                            tensor.wait_ge(ev, EVA(bi - 1, 1, 1))
                        else:
                            tensor.wait_ge(ev, EVA(bi, 0, 1))
                    for hc in range(NH):
                        tensor.matmul((ps_d0 if eh == 0 else ps_d1)[:, :],
                                      hqT[:, hc * TB + tc * 128: hc * TB + tc * 128 + 128],
                                      wdqT[:, hc * D + eh * 512: hc * D + eh * 512 + 512],
                                      start=(hc == 0), stop=(hc == NH - 1))
                    tensor.matmul(ps_x[:, :], idn[:, :], idn[:, :],
                                  start=True, stop=True).then_inc(mmd, 1)

    def _fn_gpsimd(gpsimd):
        for bi in range(NB):
            for tc in range(2):
                gpsimd.wait_ge(ev, EVA(bi, tc, 5))
                gpsimd.dma_start(out=x2_d[bi * TB + tc * 128: bi * TB + (tc + 1) * 128, :],
                                 in_=o_f[:, tc * D:(tc + 1) * D]
                                 ).then_inc(st0 if tc == 0 else st1, 16)
        gpsimd.wait_ge(st0, 16 * NB)
        gpsimd.wait_ge(st1, 16 * NB)
        if _os.environ.get("K_DBG"):
            gpsimd.dma_start(out=dbg_xq_d[:, :], in_=xq[:, :]).then_inc(st0, 16)
            gpsimd.dma_start(out=dbg_h_d[:, :], in_=h_b[:, :]).then_inc(st0, 16)
            gpsimd.dma_start(out=dbg_hq_d[:, :], in_=hq[:, :]).then_inc(st0, 16)
            gpsimd.dma_start(out=dbg_xqT_d[:, :], in_=xqT[:, :]).then_inc(st0, 16)
            gpsimd.dma_start(out=dbg_hqT_d[:, :], in_=hqT[:, :]).then_inc(st0, 16)
            gpsimd.dma_start(out=dbg_wg_d[:, :], in_=wgqT[:, :]).then_inc(st0, 16)
            gpsimd.dma_start(out=dbg_wu_d[:, :], in_=wuqT[:, :]).then_inc(st1, 16)
            if NB == 1:
                gpsimd.wait_ge(ev, 10 * NB + 1)
                gpsimd.dma_start(out=dbg_ps_d[:, :], in_=lns[:, 0:1024]).then_inc(st0, 16)
                gpsimd.dma_start(out=dbg_sg_d[:, :], in_=o_f[:, 0:1024]).then_inc(st0, 16)
                gpsimd.wait_ge(st0, 16 * NB + 128)
            else:
                gpsimd.wait_ge(st0, 16 * NB + 96)

    fns = {'sy': _fn_sync, 'v': _fn_vector, 's': _fn_scalar,
           't': _fn_tensor, 'g': _fn_gpsimd, 'ps_x': ps_x, 'idn': None}
    if sb_ is not None:
        return fns
    blk = b.block()

    @blk.sync
    def _(sync):
        fns['sy'](sync)

    @blk.vector
    def _(vector):
        fns['v'](vector)

    @blk.scalar
    def _(scalar):
        fns['s'](scalar)

    @blk.tensor
    def _(tensor):
        fns['t'](tensor)

    @blk.gpsimd
    def _(gpsimd):
        fns['g'](gpsimd)

    return b.done()


def run_launch_B(x1_full, wg, wu, wd):
    nc = build_launch_B()
    x1 = np.ascontiguousarray(x1_full.reshape(NCORES * T, D))
    idn = np.eye(128, dtype=ml_dtypes.bfloat16)
    in_maps = []
    for c in range(NCORES):
        xs = np.ascontiguousarray(x1[c * T:(c + 1) * T])
        in_maps.append({"x1": xs, "wg": wg, "wu": wu, "wd": wd, "idn": idn})
    res = run_bass_kernel_spmd(nc, in_maps, core_ids=list(range(NCORES)))
    x2 = np.concatenate([r["x2"] for r in res.results], 0)
    return x2.reshape(4, 8192, D)

SHIFT = 64.0


def build_launch_A(sb_=None):
    b = sb_ or Builder()
    import os as _os
    L = 8192
    NTB = L // 512
    NQB = T // 256
    NKT = L // 128
    if _os.environ.get("K_NQB"):
        NQB = int(_os.environ["K_NQB"])
    xT_d = b.inp("xT", [D, L], F32)
    xo_d = b.inp("xo", [T, D], F32)
    wqT_d = b.inp("WqT", [D, D], F32)
    wkT_d = b.inp("WkT", [D, D], F32)
    wvT_d = b.inp("WvT", [D, D], F32)
    hf_d = b.inp("hf", [1, 1], F32)
    x1_d = b.out("x1", [T, D], F32)
    if _os.environ.get("K_DBG"):
        kdbg_d = b.out("kdbg", [D, 512], BF16)
        vdbg_d = b.out("vdbg", [512, 1032], BF16)
    kT_s = b.scratch("kT_s", [D, L], BF16)
    v_s = b.scratch("v_s", [L, 1032], BF16)
    HF = int(_os.environ.get("K_HALF", "0"))


    w_bf = b.sb("w_bf", [128, 3 * 8 * D], BF16)
    wstg = b.sb("wstg", [128, 2 * D], F32)
    xstg = b.sb("xstg", [128, 8 * 512], F32)
    xTb = b.sb("xTb", [128, 8 * 512], BF16)
    qT = b.sb("qT", [128, 8 * T], BF16)
    kTblk = b.sb("kTblk", [128, 8 * 512], BF16)
    vblk = b.sb("vblk", [128, 4 * 1032], BF16)
    kTt = b.sb("kTt", [128, 2 * 1024], BF16)
    vt = b.sb("vt", [128, 2 * 1032], BF16)
    pT = b.sb("pT", [128, 2 * 256], BF16)
    o_f = b.sb("o_f", [128, 2 * D], F32)
    xr = b.sb("xr", [128, 2 * D], F32)
    rs_c = b.sb("rs_c", [128, 4], F32)
    lns = b.sb("lns", [128, D], F32)
    lnc = b.sb("lnc", [128, 8], F32)
    cpool = b.sb("cpool", [128, 4], F32)

    psA = [b.ps(f"psA{i}", [128, 512]) for i in range(4)]
    psL = b.ps("psL", [128, 256])
    psS = [b.ps(f"psS{i}", [128, 8]) for i in range(2)]
    ps_x = b.ps("ps_x", [128, 128])

    ld = b.sem("ld")
    wcv = b.sem("wcv")
    xcv = b.sem("xcv")
    mmp = b.sem("mmp")
    evp = b.sem("evp")
    skv = b.sem("skv")
    lkv0 = b.sem("lkv0")
    lkv1 = b.sem("lkv1")
    mml = b.sem("mml")
    pexp = b.sem("pexp")
    mma = b.sem("mma")
    lres = b.sem("lres")
    ev = b.sem("ev")
    st0 = b.sem("st0")
    st1 = b.sem("st1")
    vse = b.sem("vse")

    def groups_in_tb(tb):
        return 16 + (8 if tb < 8 else 0)
    GRP_BASE = [0]
    for tb in range(NTB):
        GRP_BASE.append(GRP_BASE[-1] + groups_in_tb(tb))
    NGRP = GRP_BASE[-1]

    def EVA(bi, tc, k):
        return 10 * bi + 5 * tc + k


    def _fn_sync(sync):
        for wi in range(24):
            if wi >= 2:
                sync.wait_ge(wcv, wi - 1)
            src = (wqT_d, wkT_d, wvT_d)[wi // 8]
            ec = wi % 8
            sync.dma_start(out=wstg[:, (wi % 2) * D:(wi % 2 + 1) * D],
                           in_=src[ec * 128:(ec + 1) * 128, :]).then_inc(ld, 16)
        for tb in range(NTB):
            if tb >= 1:
                sync.wait_ge(xcv, tb)
            for ec in range(8):
                sync.dma_start(out=xstg[:, ec * 512:(ec + 1) * 512],
                               in_=xT_d[ec * 128:(ec + 1) * 128, tb * 512:(tb + 1) * 512]
                               ).then_inc(ld, 16)
        for qb in range(NQB):
            for tc in range(2):
                if qb >= 1:
                    sync.wait_ge(ev, 10 * qb)
                sync.dma_start(out=xr[:, tc * D:(tc + 1) * D],
                               in_=xo_d[qb * 256 + tc * 128: qb * 256 + (tc + 1) * 128, :]
                               ).then_inc(lres, 16)
            for kt in range(NKT):
                par = kt % 2
                if qb == 0 and kt < 2:
                    sync.wait_ge(skv, 32 * NTB)
                gk = qb * NKT + kt
                if gk >= 2:
                    sync.wait_ge(mma, gk - 1)
                sync.dma_start(
                    out=kTt[:, par * 1024:(par + 1) * 1024].rearrange("p (c t) -> p c t", c=8),
                    in_=kT_s[:, kt * 128:(kt + 1) * 128].rearrange("(c p) t -> p c t", p=128)
                    ).then_inc(lkv0 if par == 0 else lkv1, 16)
                sync.dma_start(
                    out=vt[:, par * 1032:(par + 1) * 1032],
                    in_=v_s[kt * 128:(kt + 1) * 128, :]).then_inc(lkv0 if par == 0 else lkv1, 16)

    def _fn_gpsimd(gpsimd):
        for tb in range(NTB):
            gpsimd.wait_ge(evp, GRP_BASE[tb] + 8)
            gpsimd.dma_start(
                out=kT_s[:, tb * 512:(tb + 1) * 512].rearrange("(c p) t -> p c t", p=128),
                in_=kTblk[:, :].rearrange("p (c t) -> p c t", c=8)).then_inc(skv, 16)
            gpsimd.wait_ge(evp, GRP_BASE[tb] + 16)
            gpsimd.dma_start(
                out=v_s[tb * 512:(tb + 1) * 512, :].rearrange("(s p) c -> p s c", p=128),
                in_=vblk[:, :].rearrange("p (s c) -> p s c", s=4)).then_inc(skv, 16)
        for qb in range(NQB):
            for tc in range(2):
                gpsimd.wait_ge(ev, EVA(qb, tc, 5))
                gpsimd.dma_start(out=x1_d[qb * 256 + tc * 128: qb * 256 + (tc + 1) * 128, :],
                                 in_=o_f[:, tc * D:(tc + 1) * D]
                                 ).then_inc(st0 if tc == 0 else st1, 16)
        gpsimd.wait_ge(st0, 16 * NQB)
        gpsimd.wait_ge(st1, 16 * NQB)
        gpsimd.wait_ge(skv, 32 * NTB)
        if _os.environ.get("K_DBG"):
            gpsimd.dma_start(out=kdbg_d[:, :], in_=kT_s[:, 0:512]).then_inc(st0, 16)
            gpsimd.dma_start(out=vdbg_d[:, :], in_=v_s[0:512, :]).then_inc(st0, 16)
            gpsimd.wait_ge(st0, 16 * NQB + 32)

    def _fn_vector(vector):
        nse = [0]
        def edge(ins):
            nse[0] += 1
            ins.then_inc(vse, 1)
            vector.wait_ge(vse, nse[0])
        vector.memset(cpool[:, 0:1], LN_EPS)
        vector.memset(cpool[:, 1:2], -SHIFT)
        on3 = vblk[:].rearrange("p (s c) -> p s c", c=1032)
        vector.memset(on3[:, :, 1024:1032], 1.0)
        for wi in range(24):
            vector.wait_ge(ld, 16 * (wi + 1))
            vector.tensor_copy(w_bf[:, wi * D:(wi + 1) * D],
                               wstg[:, (wi % 2) * D:(wi % 2 + 1) * D]).then_inc(wcv, 1)
        for tb in range(NTB):
            vector.wait_ge(ld, 16 * 24 + 128 * (tb + 1))
            if tb >= 1:
                vector.wait_ge(mmp, GRP_BASE[tb])
            vector.tensor_copy(xTb[:, :], xstg[:, :]).then_inc(xcv, 1)
            g0 = GRP_BASE[tb]
            for c in range(8):
                vector.wait_ge(mmp, g0 + c + 1)
                if tb >= 1 and c == 0:
                    vector.wait_ge(skv, 32 * tb)
                vector.tensor_copy(kTblk[:, c * 512:(c + 1) * 512], psA[0][:, :]
                                   ).then_inc(evp, 1)
            for i in range(8):
                tsub, eh = i // 2, i % 2
                vector.wait_ge(mmp, g0 + 8 + i + 1)
                vector.tensor_copy(
                    vblk[:, tsub * 1032 + eh * 512: tsub * 1032 + eh * 512 + 512],
                    psA[1 + eh][:, :]).then_inc(evp, 1)
            if tb < 8:
                for c in range(8):
                    vector.wait_ge(mmp, g0 + 16 + c + 1)
                    vector.tensor_copy(qT[:, c * T + tb * 512: c * T + (tb + 1) * 512],
                                       psA[3][:, :]).then_inc(evp, 1)
        for qb in range(NQB):
            vector.wait_ge(mma, qb * NKT + NKT)
            vector.wait_ge(lres, 32 * (qb + 1))
            for qc in range(2):
                edge(vector.reciprocal(rs_c[:, qc:qc + 1], psS[qc][:, 0:1]))
                edge(vector.tensor_scalar_mul(rs_c[:, 2 + qc:3 + qc],
                                              rs_c[:, qc:qc + 1], 1.0 / 32.0))
            for tc in range(2):
                for eh in range(2):
                    vector.scalar_tensor_tensor(
                        o_f[:, tc * D + eh * 512: tc * D + eh * 512 + 512],
                        psA[2 * tc + eh][:, :], rs_c[:, 2 + tc:3 + tc],
                        xr[:, tc * D + eh * 512: tc * D + eh * 512 + 512],
                        ALU.mult, ALU.add)
                o = o_f[:, tc * D:(tc + 1) * D]
                vector.reduce_sum(lnc[:, 0:1], o, AX.X).then_inc(ev, 1)
                vector.wait_ge(ev, EVA(qb, tc, 2))
                vector.tensor_scalar(lnc[:, 2:3], lnc[:, 0:1], 1.0 / D, None, ALU.mult)
                vector.tensor_scalar(lnc[:, 3:4], lnc[:, 1:2], 1.0 / D, None, ALU.mult)
                edge(vector.tensor_tensor(lnc[:, 4:5], lnc[:, 2:3], lnc[:, 2:3],
                                          ALU.mult))
                vector.tensor_sub(lnc[:, 5:6], lnc[:, 3:4], lnc[:, 4:5]
                                  ).then_inc(ev, 1)
                vector.wait_ge(ev, EVA(qb, tc, 4))
                edge(vector.reciprocal(lnc[:, 7:8], lnc[:, 6:7]))
                if qb >= 1:
                    vector.wait_ge(st0 if tc == 0 else st1, 16 * qb)
                vector.tensor_scalar(o, o, lnc[:, 2:3], lnc[:, 7:8],
                                     ALU.subtract, ALU.mult).then_inc(ev, 1)

    def _fn_scalar(scalar):
        for qb in range(NQB):
            for kt in range(NKT):
                gk = qb * NKT + kt
                scalar.wait_ge(mml, gk + 1)
                if gk >= 2:
                    scalar.wait_ge(mma, gk - 1)
                scalar.activation(pT[:, (kt % 2) * 256:(kt % 2 + 1) * 256],
                                  psL[:, :], AF.Exp, bias=cpool[:, 1:2]
                                  ).then_inc(pexp, 1)
            for tc in range(2):
                scalar.wait_ge(ev, EVA(qb, tc, 1))
                o = o_f[:, tc * D:(tc + 1) * D]
                scalar.activation(lns[:], o, AF.Square, accum_out=lnc[:, 1:2]
                                  ).then_inc(ev, 1)
                scalar.wait_ge(ev, EVA(qb, tc, 3))
                scalar.activation(lnc[:, 6:7], lnc[:, 5:6], AF.Sqrt, bias=cpool[:, 0:1]
                                  ).then_inc(ev, 1)

    def _fn_tensor(tensor):
        for tb in range(NTB):
            tensor.wait_ge(xcv, tb + 1)
            g0 = GRP_BASE[tb]
            g = g0
            for c in range(8):
                if g >= 1:
                    tensor.wait_ge(evp, g)
                for ec in range(8):
                    if tb == 0 and c == 0:
                        tensor.wait_ge(wcv, 8 + ec + 1)
                    tensor.matmul(psA[0][:, :],
                                  w_bf[:, (8 + ec) * D + c * 128: (8 + ec) * D + c * 128 + 128],
                                  xTb[:, ec * 512:(ec + 1) * 512],
                                  start=(ec == 0), stop=(ec == 7))
                tensor.matmul(ps_x[:, :], w_bf[:, 0:128], xTb[:, 0:128],
                              start=True, stop=True).then_inc(mmp, 1)
                g += 1
            for i in range(8):
                tsub, eh = i // 2, i % 2
                if g >= 1:
                    tensor.wait_ge(evp, g)
                for ec in range(8):
                    if tb == 0 and i == 0:
                        tensor.wait_ge(wcv, 16 + ec + 1)
                    tensor.matmul(psA[1 + eh][:, :],
                                  xTb[:, ec * 512 + tsub * 128: ec * 512 + tsub * 128 + 128],
                                  w_bf[:, (16 + ec) * D + eh * 512: (16 + ec) * D + eh * 512 + 512],
                                  start=(ec == 0), stop=(ec == 7))
                tensor.matmul(ps_x[:, :], w_bf[:, 0:128], xTb[:, 0:128],
                              start=True, stop=True).then_inc(mmp, 1)
                g += 1
            if tb < 8:
                for c in range(8):
                    if g >= 1:
                        tensor.wait_ge(evp, g)
                    for ec in range(8):
                        if tb == 0 and c == 0:
                            tensor.wait_ge(wcv, ec + 1)
                        tensor.matmul(psA[3][:, :],
                                      w_bf[:, ec * D + c * 128: ec * D + c * 128 + 128],
                                      xTb[:, ec * 512:(ec + 1) * 512],
                                      start=(ec == 0), stop=(ec == 7))
                    tensor.matmul(ps_x[:, :], w_bf[:, 0:128], xTb[:, 0:128],
                                  start=True, stop=True).then_inc(mmp, 1)
                    g += 1
        nl = [0, 0]
        for qb in range(NQB):
            for kt in range(NKT):
                par = kt % 2
                nl[par] += 32
                gk = qb * NKT + kt
                tensor.wait_ge(lkv0 if par == 0 else lkv1, nl[par])
                P2 = _os.environ.get("K_P2", "full")
                if P2 in ("full", "logits", "noatt"):
                    for c in range(0 if _os.environ.get("K_NOLG") else 8):
                        tensor.matmul(psL[:, :],
                                      kTt[:, par * 1024 + c * 128: par * 1024 + c * 128 + 128],
                                      qT[:, c * T + qb * 256: c * T + qb * 256 + 256],
                                      start=(c == 0), stop=(c == 7))
                tensor.matmul(ps_x[:, :], w_bf[:, 0:128], w_bf[:, 0:128],
                              start=True, stop=True).then_inc(mml, 1)
                tensor.wait_ge(pexp, gk + 1)
                if kt == 0 and qb >= 1:
                    tensor.wait_ge(ev, EVA(qb - 1, 1, 1))
                last = None
                for qc in range(2):
                    if P2 in ("full", "noatt"):
                        for eh in range(2):
                            last = tensor.matmul(
                                psA[2 * qc + eh][:, :],
                                pT[:, par * 256 + qc * 128: par * 256 + qc * 128 + 128],
                                vt[:, par * 1032 + eh * 512: par * 1032 + eh * 512 + 512],
                                start=(kt == 0), stop=(kt == NKT - 1)) if P2 == "full" else \
                                tensor.matmul(
                                psA[2 * qc + eh][:, :],
                                pT[:, par * 256 + qc * 128: par * 256 + qc * 128 + 128],
                                vt[:, par * 1032: par * 1032 + 512],
                                start=True, stop=True)
                    if P2 == "full":
                        last = tensor.matmul(
                            psS[qc][:, :],
                            pT[:, par * 256 + qc * 128: par * 256 + qc * 128 + 128],
                            vt[:, par * 1032 + 1024: par * 1032 + 1032],
                            start=(kt == 0), stop=(kt == NKT - 1))
                if kt == NKT - 1 or P2 != "full":
                    tensor.matmul(ps_x[:, :], w_bf[:, 0:128], w_bf[:, 0:128],
                                  start=True, stop=True).then_inc(mma, 1)
                else:
                    last.then_inc(mma, 1)

    fns = {'sy': _fn_sync, 'v': _fn_vector, 's': _fn_scalar,
           't': _fn_tensor, 'g': _fn_gpsimd, 'ps_x': ps_x, 'idn': None}
    if sb_ is not None:
        return fns
    blk = b.block()

    @blk.sync
    def _(sync):
        fns['sy'](sync)

    @blk.vector
    def _(vector):
        fns['v'](vector)

    @blk.scalar
    def _(scalar):
        fns['s'](scalar)

    @blk.tensor
    def _(tensor):
        fns['t'](tensor)

    @blk.gpsimd
    def _(gpsimd):
        fns['g'](gpsimd)

    return b.done()


def run_launch_A(x_full, Wq, Wk, Wv):
    nc = build_launch_A()
    wqT = np.ascontiguousarray(Wq.T)
    wkT = np.ascontiguousarray(Wk.T)
    wvT = np.ascontiguousarray(Wv.T)
    in_maps = []
    for c in range(NCORES):
        bb, hf = c // 2, c % 2
        xb = x_full[bb]
        own = xb[hf * T:(hf + 1) * T]
        other = xb[(1 - hf) * T:(2 - hf) * T]
        xperm = np.concatenate([own, other], 0)
        in_maps.append({"xT": np.ascontiguousarray(xperm.T),
                        "xo": np.ascontiguousarray(own),
                        "WqT": wqT, "WkT": wkT, "WvT": wvT,
                        "hf": np.zeros((1, 1), np.float32)})
    res = run_bass_kernel_spmd(nc, in_maps, core_ids=list(range(NCORES)))
    x1 = np.zeros((4, 8192, D), np.float32)
    for c in range(NCORES):
        bb, hf = c // 2, c % 2
        x1[bb, hf * T:(hf + 1) * T] = res.results[c]["x1"]
    return x1


class Em:
    def __init__(self, b, sems, active=None, engs=None):
        self.b = b
        self.active = active
        self.engs = engs or {}
        self.sems = sems
        self.n = {k: 0 for k in sems}

    def _emit(self, eng, fn, mark=None):
        if self.active == eng:
            ins = fn(self.engs[eng])
            if mark:
                ins.then_inc(self.sems[mark], 1)
        if mark:
            self.n[mark] += 1

    def v(self, fn, mark=None):
        self._emit('v', fn, mark)

    def s(self, fn, mark=None):
        self._emit('s', fn, mark)

    def t(self, fn, mark=None):
        self._emit('t', fn, mark)

    def g(self, fn, mark=None):
        self._emit('g', fn, mark)

    def sy(self, fn, mark=None):
        self._emit('sy', fn, mark)

    def dma(self, eng, fn, mark):
        if self.active == eng:
            fn(self.engs[eng]).then_inc(self.sems[mark], 16)
        self.n[mark] += 16

    def wait(self, eng, mark, val=None):
        if val is None:
            val = self.n[mark]
        if self.active == eng and val > 0:
            self.engs[eng].wait_ge(self.sems[mark], val)

    def edge(self, eng, mark):
        self.wait(eng, mark)


def build_launch_C1(sb_=None):
    import os as _os
    SEG = 1024
    NSEG = T // SEG
    NCH = SEG // 64
    if _os.environ.get("K_NSEG"):
        NSEG = int(_os.environ["K_NSEG"])
    if _os.environ.get("K_NCH"):
        NCH = int(_os.environ["K_NCH"])
    PI = float(np.pi)

    b = sb_ or Builder()
    x2T_d = b.inp("x2T", [D, T], F32)
    winT_d = b.inp("WinT", [D, D], F32)
    wqkT_d = b.inp("WqkT", [D, 2048], F32)
    binp_d = b.inp("binP", [128, 8], F32)
    bqkP_d = b.inp("bqkP", [128, 16], F32)
    mkc_d = b.inp("mkc", [128, 8], F32)
    mk64_d = b.inp("mk64", [64, 64], U8)
    bdm_d = b.inp("bdm", [128, 128], F32)
    idn_d = b.inp("idn", [128, 128], BF16)
    yloc_d = b.out("yloc", [T, 512], F32)
    w2r_d = b.out("W2r", [256, T], F32)
    w2i_d = b.out("W2i", [256, T], F32)
    sfin_d = b.out("Sfin", [256, 64], F32)
    ztot_d = b.out("Ztot", [256, 2], F32)
    if _os.environ.get("K_DBGC"):
        dbgc_d = b.out("dbgc", [128, 384], F32)

    win_b = b.sb("win_b", [128, 8 * D], BF16)
    wqk_b = b.sb("wqk_b", [128, 8 * 2048], BF16)
    wstg = b.sb("wstg", [128, 2048], F32)
    binP = b.sb("binP_s", [128, 8], F32)
    bqkP = b.sb("bqkP_s", [128, 16], F32)
    mkc = b.sb("mkc_s", [128, 8], F32)
    mk64 = b.sb("mk64_s", [64, 64], U8)
    bdm = b.sb("bdm_s", [128, 128], F32)
    idn = b.sb("idn_s", [128, 128], BF16)
    xstg = b.sb("xstg", [128, 2048], F32)
    x2Tb = b.sb("x2Tb", [128, 8 * SEG], BF16)
    uT = b.sb("uT", [128, 8 * SEG], BF16)
    qr_p = b.sb("qr_p", [128, 2 * SEG], BF16)
    qi_p = b.sb("qi_p", [128, 2 * SEG], BF16)
    kr_p = b.sb("kr_p", [128, 2 * SEG], BF16)
    ki_p = b.sb("ki_p", [128, 2 * SEG], BF16)
    ar_p = b.sb("ar_p", [128, 2 * SEG], F32)
    ai_p = b.sb("ai_p", [128, 2 * SEG], F32)
    vT_p = b.sb("vT_p", [128, 4 * SEG], BF16)
    v_tok = b.sb("v_tok", [128, 8 * 768], BF16)
    v_swp = b.sb("v_swp", [128, 8 * 768], BF16)
    lam_p = b.sb("lam_p", [128, 2 * SEG], F32)
    th_p = b.sb("th_p", [128, 2 * SEG], F32)
    cos_p = b.sb("cos_p", [128, 2 * SEG], BF16)
    sin_p = b.sb("sin_p", [128, 2 * SEG], BF16)
    Mc = b.sb("Mc", [128, 4 * NCH], F32)
    eMc = b.sb("eMc", [128, 4 * NCH], F32)
    zcol = b.sb("zcol", [128, 64], F32)
    ta = b.sb("ta", [128, 64], F32)
    tb_ = b.sb("tb_", [128, 64], F32)
    tc_ = b.sb("tc_", [128, 64], F32)
    td_ = b.sb("td_", [128, 64], F32)
    ceE = b.sb("ceE", [128, 64], F32)
    seE = b.sb("seE", [128, 64], F32)
    qtP1 = b.sb("qtP1", [128, 128], BF16)
    qtP2 = b.sb("qtP2", [128, 128], BF16)
    kt_r = b.sb("kt_r", [128, 64], F32)
    kt_i = b.sb("kt_i", [128, 64], F32)
    k2r = b.sb("k2r", [128, 64], BF16)
    k2i = b.sb("k2i", [128, 64], BF16)
    q2r_f = b.sb("q2r_f", [128, 64], F32)
    q2i_f = b.sb("q2i_f", [128, 64], F32)
    w2r_b = b.sb("w2r_b", [128, 2 * 64], F32)
    w2i_b = b.sb("w2i_b", [128, 2 * 64], F32)
    skr = [b.sb(f"skr{i}", [128, 64], BF16) for i in range(2)]
    ski = [b.sb(f"ski{i}", [128, 64], BF16) for i in range(2)]
    sq2r = [b.sb(f"sq2r{i}", [128, 64], BF16) for i in range(2)]
    sq2i = [b.sb(f"sq2i{i}", [128, 64], BF16) for i in range(2)]
    sq2n = [b.sb(f"sq2n{i}", [128, 64], BF16) for i in range(2)]
    Sm_r = b.sb("Sm_r", [128, 64], BF16)
    Sm_i = b.sb("Sm_i", [128, 64], BF16)
    k2tok_r = b.sb("k2tok_r", [128, 128], BF16)
    k2tok_i = b.sb("k2tok_i", [128, 128], BF16)
    Hr = b.sb("Hr", [128, 2 * 32], F32)
    Hi = b.sb("Hi", [128, 2 * 32], F32)
    Hrb = b.sb("Hrb", [128, 2 * 32], BF16)
    Hib = b.sb("Hib", [128, 2 * 32], BF16)
    Zr = b.sb("Zr", [128, 2], F32)
    Zi = b.sb("Zi", [128, 2], F32)
    Dc = b.sb("Dc", [128, 16], F32)
    kXr = b.sb("kXr", [128, 32], BF16)
    kXi = b.sb("kXi", [128, 32], BF16)
    skrX = [b.sb(f"skrX{i}", [128, 32], BF16) for i in range(2)]
    skiX = [b.sb(f"skiX{i}", [128, 32], BF16) for i in range(2)]
    skrB = [b.sb(f"skrB{i}", [128, 64], BF16) for i in range(2)]
    skiB = [b.sb(f"skiB{i}", [128, 64], BF16) for i in range(2)]
    mtmp = b.sb("mtmp", [128, 128], F32)
    y_sb = b.sb("y_sb", [128, 512], F32)
    hout = b.sb("hout", [128, 2 * 64], F32)
    cpool = b.sb("cpool", [128, 8], F32)

    ps_a = b.ps("ps_a", [128, 512])
    ps_b = b.ps("ps_b", [128, 512])
    ps_y = b.ps("ps_y", [128, 512])
    ps_hr = b.ps("ps_hr", [128, 128])
    ps_hi = b.ps("ps_hi", [128, 128])
    ps_t = b.ps("ps_t", [128, 1024], BF16)
    ps_x = b.ps("ps_x", [128, 128])

    semnames = ["ld", "vm", "sm", "tm", "gm", "xcv", "wcv"]
    sems = {k: b.sem(k) for k in semnames}

    dpA = x2Tb[:].bitcast(F32)[:, 0:2048]
    dpB = x2Tb[:].bitcast(F32)[:, 2048:4096]
    eP_p = ar_p
    eM_p = ai_p

    NW = 8 + 8

    def script(E):
        def mset(ap, val):
            E.v(lambda v: v.memset(ap, val))
        mset(cpool[:, 0:1], LN_EPS); mset(cpool[:, 1:2], M23)
        mset(cpool[:, 2:3], 2.0); mset(cpool[:, 3:4], -float(2 ** 22))
        mset(cpool[:, 4:5], 0.5); mset(cpool[:, 5:6], 1.0 / (2 * PI))
        mset(cpool[:, 6:7], -1.0); mset(cpool[:, 7:8], -60.0)
        mset(zcol[:], 0.0); mset(Hr[:], 0.0); mset(Hi[:], 0.0)
        mset(Hrb[:], 0.0); mset(Hib[:], 0.0)
        mset(Zr[:], 1.0); mset(Zi[:], 0.0)
        mset(Sm_r[:], 0.0); mset(Sm_i[:], 0.0)
        mset(skrB[0][:], 0.0); mset(skiB[0][:], 0.0)
        mset(skrB[1][:], 0.0); mset(skiB[1][:], 0.0)
        mset(k2tok_r[:], 0.0)
        E.v(lambda v: v.memset(k2tok_i[:], 0.0), mark="vm")

        for dst, srcd in ((binP, binp_d), (bqkP, bqkP_d), (mkc, mkc_d),
                          (mk64, mk64_d), (bdm, bdm_d), (idn, idn_d)):
            E.dma('sy', lambda sy, d=dst, s=srcd: sy.dma_start(out=d[:, :], in_=s[:, :]), "ld")
        for wi in range(NW):
            w = D if wi < 8 else 2048
            sl = 0
            srcd = winT_d[wi * 128:(wi + 1) * 128, :] if wi < 8 else \
                wqkT_d[(wi - 8) * 128:(wi - 8 + 1) * 128, :]
            E.wait('sy', 'wcv', max(0, wi))
            E.dma('sy', lambda sy, s=srcd, o=wstg[:, sl:sl + w]: sy.dma_start(out=o, in_=s), "ld")
            E.wait('v', 'ld')
            if wi < 8:
                dst = win_b[:, wi * D:(wi + 1) * D]
            else:
                dst = wqk_b[:, (wi - 8) * 2048:(wi - 8 + 1) * 2048]
            E.v(lambda v, d=dst, s=wstg[:, sl:sl + w]: v.tensor_copy(d, s), mark="wcv")

        def tgroup(mms, mark="tm"):
            for f in mms:
                E.t(f)
            E.t(lambda t: t.matmul(ps_x[:, :], idn[:, :], idn[:, :],
                                   start=True, stop=True), mark=mark)

        for sg in range(NSEG):
            for ec in range(8):
                E.wait('sy', 'xcv', E.n['xcv'] - 1 if ec == 0 and sg >= 1 else E.n['xcv'] - 0 if False else max(0, E.n['xcv'] - (1 if ec == 0 and sg >= 1 else 0)))
                E.dma('sy', lambda sy, e=ec: sy.dma_start(
                    out=xstg[:, (e % 2) * 1024:(e % 2) * 1024 + 1024],
                    in_=x2T_d[e * 128:(e + 1) * 128, sg * SEG:(sg + 1) * SEG]), "ld")
                E.wait('v', 'ld')
                if ec >= 2:
                    E.wait('sy', 'xcv', E.n['xcv'] - 1)
                E.v(lambda v, e=ec: v.tensor_copy(
                    x2Tb[:, e * SEG:(e + 1) * SEG],
                    xstg[:, (e % 2) * 1024:(e % 2) * 1024 + 1024]), mark="xcv")
            E.wait('t', 'xcv')
            E.wait('t', 'wcv', 8)
            for dc in range(8):
                for th in range(2):
                    E.wait('t', 'sm')
                    tgroup([
                        (lambda t, e=ec2, d=dc, h=th: t.matmul(
                            ps_a[:, :],
                            win_b[:, e * D + d * 128: e * D + d * 128 + 128],
                            x2Tb[:, e * SEG + h * 512: e * SEG + h * 512 + 512],
                            start=(e == 0), stop=(e == 7)))
                        for ec2 in range(8)])
                    E.wait('s', 'tm')
                    E.s(lambda s, d=dc, h=th: s.activation(
                        uT[:, d * SEG + h * 512: d * SEG + h * 512 + 512],
                        ps_a[:, :], AF.Silu, bias=binP[:, d:d + 1]), mark="sm")
            E.wait('t', 'wcv', 16)
            planes = [(qr_p, BF16), (qi_p, BF16), (kr_p, BF16), (ki_p, BF16),
                      (ar_p, F32), (ai_p, F32)]
            for qc in range(16):
                for th in range(2):
                    E.wait('t', 'sm')
                    tgroup([
                        (lambda t, d=dc2, q=qc, h=th: t.matmul(
                            ps_b[:, :],
                            wqk_b[:, d * 2048 + q * 128: d * 2048 + q * 128 + 128],
                            uT[:, d * SEG + h * 512: d * SEG + h * 512 + 512],
                            start=(d == 0), stop=(d == 7)))
                        for dc2 in range(8)])
                    E.wait('s', 'tm')
                    comp, g = qc // 2, qc % 2
                    if comp < 6:
                        dst = planes[comp][0][:, g * SEG + th * 512: g * SEG + th * 512 + 512]
                    else:
                        vcomp = comp - 6
                        dst = vT_p[:, (vcomp * 2 + g) * SEG + th * 512:
                                   (vcomp * 2 + g) * SEG + th * 512 + 512]
                    E.s(lambda s, d=dst, q=qc: s.activation(
                        d, ps_b[:, :], AF.Identity, bias=bqkP[:, q:q + 1]), mark="sm")
            for tc in range(8):
                for pc in range(4):
                    E.wait('t', 'sm')
                    E.wait('t', 'vm')
                    E.t(lambda t, p=pc, c=tc: t.transpose(
                        ps_t[:, 0:128],
                        vT_p[:, p * SEG + c * 128: p * SEG + c * 128 + 128],
                        idn[:, :]))
                    tgroup([], mark="tm")
                    E.wait('v', 'tm')
                    comp, g = pc // 2, pc % 2
                    dst = v_tok[:, tc * 768 + comp * 256 + g * 128:
                                tc * 768 + comp * 256 + g * 128 + 128]
                    E.v(lambda v, d=dst: v.tensor_copy(d, ps_t[:, 0:128]), mark="vm")
                E.v(lambda v, c=tc: v.tensor_scalar_mul(
                    v_tok[:, c * 768 + 512: c * 768 + 768],
                    v_tok[:, c * 768 + 256: c * 768 + 512], -1.0), mark="vm")
            E.wait('g', 'vm')
            E.wait('g', 'tm')
            E.dma('g', lambda g: g.dma_start(out=v_swp[0:64, :], in_=v_tok[64:128, :]), "gm")
            E.dma('g', lambda g: g.dma_start(out=v_swp[64:128, :], in_=v_tok[0:64, :]), "gm")

            E.wait('v', 'sm')
            E.v(lambda v: v.tensor_mul(dpA[:, :], ar_p[:, :], ar_p[:, :]))
            E.v(lambda v: v.tensor_mul(dpB[:, :], ai_p[:, :], ai_p[:, :]))
            E.v(lambda v: v.tensor_add(dpA[:, :], dpA[:, :], dpB[:, :]))
            E.v(lambda v: v.tensor_scalar_add(dpB[:, :], dpA[:, :], 1.0))
            E.v(lambda v: v.reciprocal(dpB[:, :], dpB[:, :]))
            E.v(lambda v: v.tensor_mul(dpA[:, :], dpA[:, :], dpB[:, :]), mark="vm")
            E.wait('s', 'vm')
            E.s(lambda s: s.activation(lam_p[:, :], dpA[:, :], AF.Ln), mark="sm")
            E.wait('v', 'sm')
            E.v(lambda v: v.tensor_scalar_max(lam_p[:, :], lam_p[:, :], -60.0), mark="vm")
            E.s(lambda s: s.activation(dpA[:, :], ai_p[:, :], AF.Abs))
            E.s(lambda s: s.activation(dpB[:, :], ar_p[:, :], AF.Abs), mark="sm")
            E.wait('v', 'sm')
            E.v(lambda v: v.tensor_tensor(cos_p[:, :], dpA[:, :], dpB[:, :], ALU.is_gt))
            E.v(lambda v: v.tensor_single_scalar(sin_p[:, :], ar_p[:, :], 0.0, ALU.is_lt))
            E.v(lambda v: v.tensor_tensor(th_p[:, :], dpA[:, :], dpB[:, :], ALU.min))
            E.v(lambda v: v.tensor_tensor(dpA[:, :], dpA[:, :], dpB[:, :], ALU.max), mark="vm")
            E.edge('v', 'vm')
            E.v(lambda v: v.reciprocal(dpA[:, :], dpA[:, :]), mark="vm")
            E.edge('v', 'vm')
            E.v(lambda v: v.tensor_mul(th_p[:, :], th_p[:, :], dpA[:, :]), mark="vm")
            E.wait('s', 'vm')
            E.s(lambda s: s.activation(dpA[:, :], th_p[:, :], AF.Arctan))
            E.s(lambda s: s.activation(th_p[:, :], ai_p[:, :], AF.Sign), mark="sm")
            E.wait('v', 'sm')
            E.v(lambda v: v.tensor_scalar_mul(dpB[:, :], cos_p[:, :], -2.0))
            E.v(lambda v: v.tensor_scalar_add(dpB[:, :], dpB[:, :], 1.0))
            E.v(lambda v: v.tensor_mul(dpA[:, :], dpA[:, :], dpB[:, :]))
            E.v(lambda v: v.scalar_tensor_tensor(dpA[:, :], cos_p[:, :], PI / 2,
                                                 dpA[:, :], ALU.mult, ALU.add))
            E.v(lambda v: v.tensor_scalar_mul(dpB[:, :], sin_p[:, :], -2.0))
            E.v(lambda v: v.tensor_scalar_add(dpB[:, :], dpB[:, :], 1.0))
            E.v(lambda v: v.tensor_mul(dpA[:, :], dpA[:, :], dpB[:, :]))
            E.v(lambda v: v.scalar_tensor_tensor(dpA[:, :], sin_p[:, :], PI,
                                                 dpA[:, :], ALU.mult, ALU.add))
            E.v(lambda v: v.tensor_mul(th_p[:, :], dpA[:, :], th_p[:, :]), mark="vm")
            for g in range(2):
                for c in range(NCH):
                    sl_ = slice(g * SEG + c * 64, g * SEG + c * 64 + 64)
                    E.v(lambda v, s_=sl_: v.tensor_tensor_scan(
                        ta[:, :], lam_p[:, s_], zcol[:, :], zcol[:, 0:1],
                        ALU.add, ALU.add), mark="vm")
                    E.edge('v', 'vm')
                    E.v(lambda v, s_=sl_: v.tensor_copy(lam_p[:, s_], ta[:, :]),
                        mark="vm")
                    E.edge('v', 'vm')
                    E.v(lambda v, s_=sl_: v.tensor_tensor_scan(
                        ta[:, :], th_p[:, s_], zcol[:, :], zcol[:, 0:1],
                        ALU.add, ALU.add), mark="vm")
                    E.edge('v', 'vm')
                    E.v(lambda v, s_=sl_: v.tensor_copy(th_p[:, s_], ta[:, :]), mark="vm")
                    E.edge('v', 'vm')
                    E.v(lambda v, g_=g, c_=c: v.tensor_add(
                        Mc[:, 2 * (g_ * NCH + c_):2 * (g_ * NCH + c_) + 1],
                        lam_p[:, g_ * SEG + c_ * 64:g_ * SEG + c_ * 64 + 1],
                        lam_p[:, g_ * SEG + c_ * 64 + 31:g_ * SEG + c_ * 64 + 32]))
                    E.v(lambda v, g_=g, c_=c: v.tensor_add(
                        Mc[:, 2 * (g_ * NCH + c_) + 1:2 * (g_ * NCH + c_) + 2],
                        lam_p[:, g_ * SEG + c_ * 64 + 32:g_ * SEG + c_ * 64 + 33],
                        lam_p[:, g_ * SEG + c_ * 64 + 63:g_ * SEG + c_ * 64 + 64]),
                        mark="vm")
                    E.edge('v', 'vm')
            E.v(lambda v: v.tensor_scalar_mul(Mc[:, :], Mc[:, :], 0.5), mark="vm")
            E.edge('v', 'vm')
            E.wait('s', 'vm')
            E.s(lambda s: s.activation(eMc[:, :], Mc[:, :], AF.Exp), mark="sm")
            W = NCH * 64
            for g in range(2):
                gb = g * SEG
                taw = dpA[:, 0:W]
                sgw = dpA[:, 1024:1024 + W]
                absw = dpB[:, 0:W]
                mw = dpB[:, 1024:1024 + W]
                for c in range(NCH):
                    for hf in range(2):
                        co = c * 64 + hf * 32
                        mcol = Mc[:, 2 * (g * NCH + c) + hf:2 * (g * NCH + c) + hf + 1]
                        E.v(lambda v, o=co, m=mcol, t_=taw: v.tensor_scalar(
                            t_[:, o:o + 32], lam_p[:, gb + o:gb + o + 32], m, None,
                            ALU.subtract),
                            mark=("vm" if (c == NCH - 1 and hf == 1) else None))
                E.edge('v', 'vm')
                E.v(lambda v, t_=taw: v.tensor_scalar_min(t_[:, :], t_[:, :], 80.0),
                    mark="vm")
                E.edge('v', 'vm')
                E.v(lambda v, t_=taw: v.tensor_scalar_max(t_[:, :], t_[:, :], -80.0),
                    mark="vm")
                E.wait('s', 'vm')
                E.s(lambda s, t_=taw: s.activation(eP_p[:, gb:gb + W], t_[:, :],
                                                   AF.Exp))
                E.s(lambda s, t_=taw: s.activation(eM_p[:, gb:gb + W], t_[:, :],
                                                   AF.Exp, scale=cpool[:, 6:7]))
                E.s(lambda s, a_=absw: s.activation(a_[:, :], th_p[:, gb:gb + W],
                                                    AF.Abs))
                E.s(lambda s, g_=sgw: s.activation(g_[:, :], th_p[:, gb:gb + W],
                                                   AF.Sign), mark="sm")
                E.wait('v', 'sm')
                for j_ in (5, 4, 3, 2, 1, 0):
                    Tj = 2 * PI * (2 ** j_) - PI
                    Sj = -2 * PI * (2 ** j_)
                    E.v(lambda v, T=Tj, a_=absw, m_=mw: v.tensor_single_scalar(
                        m_[:, :], a_[:, :], T, ALU.is_gt), mark="vm")
                    E.edge('v', 'vm')
                    E.v(lambda v, S_=Sj, a_=absw, m_=mw: v.scalar_tensor_tensor(
                        a_[:, :], m_[:, :], S_, a_[:, :], ALU.mult, ALU.add),
                        mark="vm")
                    E.edge('v', 'vm')
                E.v(lambda v, a_=absw, g_=sgw: v.tensor_mul(a_[:, :], a_[:, :],
                                                            g_[:, :]), mark="vm")
                E.wait('s', 'vm')
                E.s(lambda s, a_=absw: s.activation(sin_p[:, gb:gb + W], a_[:, :],
                                                    AF.Sin))
                E.s(lambda s, a_=absw, g_=sgw: s.activation(g_[:, :], a_[:, :],
                                                            AF.Sin,
                                                            scale=cpool[:, 4:5]),
                    mark="sm")
                E.wait('v', 'sm')
                E.v(lambda v, g_=sgw: v.tensor_mul(g_[:, :], g_[:, :], g_[:, :]),
                    mark="vm")
                E.edge('v', 'vm')
                E.v(lambda v, g_=sgw: v.tensor_scalar_mul(g_[:, :], g_[:, :], -2.0),
                    mark="vm")
                E.edge('v', 'vm')
                E.v(lambda v, g_=sgw: v.tensor_scalar_add(cos_p[:, gb:gb + W],
                                                          g_[:, :], 1.0), mark="vm")
                E.edge('v', 'vm')

            for c in range(NCH):
                vp = v_tok if c % 2 == 0 else v_swp
                vb = (c // 2) * 768
                ro = (c % 2) * 64
                E.wait('v', 'gm')
                for g in range(2):
                    sl_ = slice(g * SEG + c * 64, g * SEG + c * 64 + 64)
                    ecol = eP_p[:, g * SEG + c * 64 + 63:g * SEG + c * 64 + 64]
                    emcol = eMc[:, 2 * (g * NCH + c) + 1:2 * (g * NCH + c) + 2]
                    emcol_a = eMc[:, 2 * (g * NCH + c):2 * (g * NCH + c) + 1]
                    mca = Mc[:, 2 * (g * NCH + c):2 * (g * NCH + c) + 1]
                    mcb = Mc[:, 2 * (g * NCH + c) + 1:2 * (g * NCH + c) + 2]
                    qP1c0 = qtP1[:, :].rearrange("p (h k) -> p h k", k=64)[:, :, 0:32]
                    qP1c1 = qtP1[:, :].rearrange("p (h k) -> p h k", k=64)[:, :, 32:64]
                    qP2c0 = qtP2[:, :].rearrange("p (h k) -> p h k", k=64)[:, :, 0:32]
                    qP2c1 = qtP2[:, :].rearrange("p (h k) -> p h k", k=64)[:, :, 32:64]
                    ta_v = ta[:, :].rearrange("p (h t) -> p h t", t=32)
                    tb_v = tb_[:, :].rearrange("p (h t) -> p h t", t=32)
                    E.v(lambda v, s_=sl_: v.tensor_mul(ceE[:, :], cos_p[:, s_], eP_p[:, s_]))
                    E.v(lambda v, s_=sl_: v.tensor_mul(seE[:, :], sin_p[:, s_], eP_p[:, s_]),
                        mark="vm")
                    E.edge('v', 'vm')
                    E.v(lambda v, s_=sl_: v.tensor_mul(ta[:, :], qr_p[:, s_], ceE[:, :]))
                    E.v(lambda v, s_=sl_: v.tensor_mul(tb_[:, :], qi_p[:, s_], seE[:, :]),
                        mark="vm")
                    E.edge('v', 'vm')
                    E.v(lambda v: v.tensor_sub(qP1c0, ta_v, tb_v))
                    E.v(lambda v, s_=sl_: v.tensor_mul(ta[:, :], qr_p[:, s_], seE[:, :]))
                    E.v(lambda v, s_=sl_: v.tensor_mul(tb_[:, :], qi_p[:, s_], ceE[:, :]),
                        mark="vm")
                    E.edge('v', 'vm')
                    E.v(lambda v: v.tensor_add(qP1c1, ta_v, tb_v), mark="vm")
                    E.edge('v', 'vm')
                    E.v(lambda v: v.tensor_scalar_mul(qP2c0, qP1c1, -1.0))
                    E.v(lambda v: v.tensor_copy(qP2c1, qP1c0))
                    E.v(lambda v, s_=sl_: v.tensor_mul(ceE[:, :], cos_p[:, s_], eM_p[:, s_]))
                    E.v(lambda v, s_=sl_: v.tensor_mul(seE[:, :], sin_p[:, s_], eM_p[:, s_]),
                        mark="vm")
                    E.edge('v', 'vm')
                    E.v(lambda v, s_=sl_: v.tensor_mul(ta[:, :], kr_p[:, s_], ceE[:, :]))
                    E.v(lambda v, s_=sl_: v.tensor_mul(tb_[:, :], ki_p[:, s_], seE[:, :]),
                        mark="vm")
                    E.edge('v', 'vm')
                    E.v(lambda v: v.tensor_add(kt_r[:, :], ta[:, :], tb_[:, :]))
                    E.v(lambda v, s_=sl_: v.tensor_mul(ta[:, :], ki_p[:, s_], ceE[:, :]))
                    E.v(lambda v, s_=sl_: v.tensor_mul(tb_[:, :], kr_p[:, s_], seE[:, :]),
                        mark="vm")
                    E.edge('v', 'vm')
                    E.v(lambda v: v.tensor_sub(kt_i[:, :], ta[:, :], tb_[:, :]))
                    E.v(lambda v, a=mca, b_=mcb: v.tensor_tensor(Dc[:, 10:11], b_, a,
                                                                 ALU.subtract), mark="vm")
                    E.wait('s', 'vm')
                    E.s(lambda s: s.activation(Dc[:, 11:12], Dc[:, 10:11], AF.Exp),
                        mark="sm")
                    E.wait('v', 'sm')
                    E.v(lambda v, e=ecol: v.tensor_tensor(Dc[:, 12:13], e, Dc[:, 11:12],
                                                          ALU.mult), mark="vm")
                    E.edge('v', 'vm')
                    E.v(lambda v, e=ecol: v.tensor_scalar(k2r[:, 32:64], kt_r[:, 32:64],
                                                          e, None, ALU.mult))
                    E.v(lambda v, e=ecol: v.tensor_scalar(k2i[:, 32:64], kt_i[:, 32:64],
                                                          e, None, ALU.mult))
                    E.v(lambda v: v.tensor_scalar(k2r[:, 0:32], kt_r[:, 0:32],
                                                  Dc[:, 12:13], None, ALU.mult))
                    E.v(lambda v: v.tensor_scalar(k2i[:, 0:32], kt_i[:, 0:32],
                                                  Dc[:, 12:13], None, ALU.mult))
                    E.v(lambda v: v.tensor_scalar(kXr[:, :], kt_r[:, 0:32],
                                                  Dc[:, 11:12], None, ALU.mult))
                    E.v(lambda v: v.tensor_scalar(kXi[:, :], kt_i[:, 0:32],
                                                  Dc[:, 11:12], None, ALU.mult))
                    E.v(lambda v, e=emcol_a: v.tensor_scalar(q2r_f[:, 0:32],
                                                             qtP1[:, 0:32], e,
                                                             None, ALU.mult))
                    E.v(lambda v, e=emcol: v.tensor_scalar(q2r_f[:, 32:64],
                                                           qtP1[:, 64:96], e,
                                                           None, ALU.mult))
                    E.v(lambda v, e=emcol_a: v.tensor_scalar(q2i_f[:, 0:32],
                                                             qtP1[:, 32:64], e,
                                                             None, ALU.mult))
                    E.v(lambda v, e=emcol: v.tensor_scalar(q2i_f[:, 32:64],
                                                           qtP1[:, 96:128], e,
                                                           None, ALU.mult), mark="vm")
                    E.edge('v', 'vm')
                    zr = Zr[:, g:g + 1]; zi = Zi[:, g:g + 1]
                    E.v(lambda v, z=zr: v.tensor_scalar(ta[:, :], q2r_f[:, :], z, None, ALU.mult))
                    E.v(lambda v, z=zi: v.tensor_scalar(tb_[:, :], q2i_f[:, :], z, None, ALU.mult),
                        mark="vm")
                    E.edge('v', 'vm')
                    E.v(lambda v, g_=g: v.tensor_sub(w2r_b[:, g_ * 64:(g_ + 1) * 64],
                                                     ta[:, :], tb_[:, :]))
                    E.v(lambda v, z=zi: v.tensor_scalar(ta[:, :], q2r_f[:, :], z, None, ALU.mult))
                    E.v(lambda v, z=zr: v.tensor_scalar(tb_[:, :], q2i_f[:, :], z, None, ALU.mult),
                        mark="vm")
                    E.edge('v', 'vm')
                    E.v(lambda v, g_=g: v.tensor_add(w2i_b[:, g_ * 64:(g_ + 1) * 64],
                                                     ta[:, :], tb_[:, :]), mark="vm")
                    E.edge('v', 'vm')
                    for hh in range(4):
                        par = hh % 2
                        mcol = mkc[:, hh:hh + 1]
                        ncol = mkc[:, 4 + hh:5 + hh]
                        E.wait('v', 'tm')
                        E.v(lambda v, m=mcol, p=par: v.tensor_scalar(
                            skr[p][:, :], kt_r[:, :], m, None, ALU.mult))
                        E.v(lambda v, m=mcol, p=par: v.tensor_scalar(
                            ski[p][:, :], kt_i[:, :], m, None, ALU.mult))
                        E.v(lambda v, m=mcol, p=par: v.tensor_scalar(
                            sq2r[p][:, :], q2r_f[:, :], m, None, ALU.mult))
                        E.v(lambda v, m=mcol, p=par: v.tensor_scalar(
                            sq2i[p][:, :], q2i_f[:, :], m, None, ALU.mult))
                        E.v(lambda v, m=mcol, p=par: v.tensor_scalar(
                            skrB[p][:, 32:64], kt_r[:, 32:64], m, None, ALU.mult))
                        E.v(lambda v, m=mcol, p=par: v.tensor_scalar(
                            skiB[p][:, 32:64], kt_i[:, 32:64], m, None, ALU.mult))
                        E.v(lambda v, m=mcol, p=par: v.tensor_scalar(
                            skrX[p][:, :], kXr[:, :], m, None, ALU.mult))
                        E.v(lambda v, m=mcol, p=par: v.tensor_scalar(
                            skiX[p][:, :], kXi[:, :], m, None, ALU.mult))
                        E.v(lambda v, m=ncol, p=par: v.tensor_scalar(
                            sq2n[p][:, :], q2i_f[:, :], m, None, ALU.mult), mark="vm")
                        E.wait('t', 'vm')
                        tgroup([
                            (lambda t, p=par: t.matmul(ps_b[0:32, 0:64], skr[p][:, 0:32],
                                                       qtP1[:, 0:64], start=True, stop=False)),
                            (lambda t, p=par: t.matmul(ps_b[0:32, 0:64], ski[p][:, 0:32],
                                                       qtP2[:, 0:64], start=False, stop=True)),
                            (lambda t, p=par: t.matmul(ps_b[0:64, 64:128], skrB[p][:, :],
                                                       qtP1[:, 64:128], start=True, stop=False)),
                            (lambda t, p=par: t.matmul(ps_b[0:64, 64:128], skiB[p][:, :],
                                                       qtP2[:, 64:128], start=False, stop=True)),
                            (lambda t, p=par: t.matmul(ps_b[0:32, 64:128], skrX[p][:, :],
                                                       qtP1[:, 64:128], start=True, stop=False)),
                            (lambda t, p=par: t.matmul(ps_b[0:32, 64:128], skiX[p][:, :],
                                                       qtP2[:, 64:128], start=False, stop=True)),
                        ])
                        E.wait('v', 'tm')
                        E.v(lambda v: v.memset(Sm_r[0:64, :], 0.0))
                        E.v(lambda v: v.memset(Sm_i[0:64, :], 0.0))
                        E.v(lambda v: v.copy_predicated(Sm_r[0:64, 0:32], mk64[:, 0:32],
                                                        ps_b[0:64, 0:32]))
                        E.v(lambda v: v.copy_predicated(Sm_r[0:64, 32:64], mk64[:, 32:64],
                                                        ps_b[0:64, 64:96]))
                        E.v(lambda v: v.copy_predicated(Sm_i[0:64, 0:32], mk64[:, 0:32],
                                                        ps_b[0:64, 32:64]))
                        E.v(lambda v: v.copy_predicated(Sm_i[0:64, 32:64], mk64[:, 32:64],
                                                        ps_b[0:64, 96:128]), mark="vm")
                        E.wait('t', 'vm')
                        h = g * 4 + hh
                        mml = []
                        for comp in range(2):
                            fo = comp * 256 + h * 32
                            if comp == 0:
                                ops = [(Sm_r[:, :], vp[:, vb + g * 128 + hh * 32: vb + g * 128 + hh * 32 + 32]),
                                       (Sm_i[:, :], vp[:, vb + 512 + g * 128 + hh * 32: vb + 512 + g * 128 + hh * 32 + 32]),
                                       (sq2r[par][:, :], Hrb[:, g * 32:(g + 1) * 32]),
                                       (sq2n[par][:, :], Hib[:, g * 32:(g + 1) * 32])]
                            else:
                                ops = [(Sm_r[:, :], vp[:, vb + 256 + g * 128 + hh * 32: vb + 256 + g * 128 + hh * 32 + 32]),
                                       (Sm_i[:, :], vp[:, vb + g * 128 + hh * 32: vb + g * 128 + hh * 32 + 32]),
                                       (sq2r[par][:, :], Hib[:, g * 32:(g + 1) * 32]),
                                       (sq2i[par][:, :], Hrb[:, g * 32:(g + 1) * 32])]
                            for oi, (lh, rh) in enumerate(ops):
                                mml.append(lambda t, lh_=lh, rh_=rh, oi_=oi, fo_=fo:
                                           t.matmul(ps_y[0:64, fo_:fo_ + 32], lh_,
                                                    rh_, start=(oi_ == 0), stop=(oi_ == 3)))
                        if hh == 3:
                            tgroup(mml)
                        else:
                            for f_ in mml:
                                E.t(f_)
                    E.wait('t', 'vm')
                    E.t(lambda t: t.transpose(ps_t[0:64, 128:256], k2r[:, :], idn[:, :]))
                    E.t(lambda t: t.transpose(ps_t[0:64, 256:384], k2i[:, :], idn[:, :]))
                    tgroup([])
                    E.wait('v', 'tm')
                    E.v(lambda v: v.tensor_copy(k2tok_r[0:64, :], ps_t[0:64, 128:256]))
                    E.v(lambda v: v.tensor_copy(k2tok_i[0:64, :], ps_t[0:64, 256:384]),
                        mark="vm")
                    E.wait('t', 'vm')
                    tgroup([
                        (lambda t: t.matmul(ps_hr[:, :], k2tok_r[:, :],
                                            vp[:, vb + g * 128: vb + g * 128 + 128],
                                            start=True, stop=False)),
                        (lambda t: t.matmul(ps_hr[:, :], k2tok_i[:, :],
                                            vp[:, vb + 512 + g * 128: vb + 512 + g * 128 + 128],
                                            start=False, stop=True)),
                        (lambda t: t.matmul(ps_hi[:, :], k2tok_r[:, :],
                                            vp[:, vb + 256 + g * 128: vb + 256 + g * 128 + 128],
                                            start=True, stop=False)),
                        (lambda t: t.matmul(ps_hi[:, :], k2tok_i[:, :],
                                            vp[:, vb + g * 128: vb + g * 128 + 128],
                                            start=False, stop=True)),
                    ])
                    E.wait('v', 'tm')
                    gsl = slice(g * 32, (g + 1) * 32)
                    E.v(lambda v: v.tensor_mul(mtmp[:, :], ps_hr[:, :], bdm[:, :]), mark="vm")
                    E.edge('v', 'vm')
                    E.v(lambda v: v.tensor_reduce(
                        hout[:, 0:32], mtmp[:].rearrange("p (h e) -> p e h", e=32),
                        AX.X, ALU.add), mark="vm")
                    E.v(lambda v: v.tensor_mul(mtmp[:, :], ps_hi[:, :], bdm[:, :]), mark="vm")
                    E.edge('v', 'vm')
                    E.v(lambda v: v.tensor_reduce(
                        hout[:, 32:64], mtmp[:].rearrange("p (h e) -> p e h", e=32),
                        AX.X, ALU.add), mark="vm")
                    E.edge('v', 'vm')
                    c63r = cos_p[:, g * SEG + c * 64 + 63:g * SEG + c * 64 + 64]
                    c63i = sin_p[:, g * SEG + c * 64 + 63:g * SEG + c * 64 + 64]
                    E.v(lambda v, e=ecol, em=emcol: v.tensor_tensor(Dc[:, 4:5], e, em,
                                                                    ALU.mult))
                    E.v(lambda v, cr=c63r: v.tensor_copy(Dc[:, 8:9], cr))
                    E.v(lambda v, ci=c63i: v.tensor_copy(Dc[:, 9:10], ci), mark="vm")
                    E.edge('v', 'vm')
                    E.v(lambda v: v.tensor_tensor(Dc[:, 0:1], Dc[:, 4:5], Dc[:, 8:9],
                                                  ALU.mult))
                    E.v(lambda v: v.tensor_tensor(Dc[:, 1:2], Dc[:, 4:5], Dc[:, 9:10],
                                                  ALU.mult), mark="vm")
                    E.edge('v', 'vm')
                    E.v(lambda v, s_=gsl: v.tensor_scalar(mtmp[:, 0:32], Hr[:, s_],
                                                          Dc[:, 0:1], None, ALU.mult))
                    E.v(lambda v, s_=gsl: v.tensor_scalar(mtmp[:, 32:64], Hi[:, s_],
                                                          Dc[:, 1:2], None, ALU.mult))
                    E.v(lambda v, s_=gsl: v.tensor_scalar(mtmp[:, 64:96], Hr[:, s_],
                                                          Dc[:, 1:2], None, ALU.mult))
                    E.v(lambda v, s_=gsl: v.tensor_scalar(mtmp[:, 96:128], Hi[:, s_],
                                                          Dc[:, 0:1], None, ALU.mult),
                        mark="vm")
                    E.edge('v', 'vm')
                    E.v(lambda v, s_=gsl: v.tensor_sub(Hr[:, s_], mtmp[:, 0:32],
                                                       mtmp[:, 32:64]))
                    E.v(lambda v, s_=gsl: v.tensor_add(Hi[:, s_], mtmp[:, 64:96],
                                                       mtmp[:, 96:128]), mark="vm")
                    E.edge('v', 'vm')
                    E.v(lambda v: v.tensor_scalar(mtmp[:, 0:32],
                                                  hout[:, 0:32], Dc[:, 8:9], None,
                                                  ALU.mult))
                    E.v(lambda v: v.tensor_scalar(mtmp[:, 32:64],
                                                  hout[:, 32:64], Dc[:, 9:10], None,
                                                  ALU.mult))
                    E.v(lambda v: v.tensor_scalar(mtmp[:, 64:96],
                                                  hout[:, 0:32], Dc[:, 9:10], None,
                                                  ALU.mult))
                    E.v(lambda v: v.tensor_scalar(mtmp[:, 96:128],
                                                  hout[:, 32:64], Dc[:, 8:9], None,
                                                  ALU.mult), mark="vm")
                    E.edge('v', 'vm')
                    E.v(lambda v: v.tensor_sub(hout[:, 0:32], mtmp[:, 0:32],
                                               mtmp[:, 32:64]))
                    E.v(lambda v: v.tensor_add(hout[:, 32:64], mtmp[:, 64:96],
                                               mtmp[:, 96:128]), mark="vm")
                    E.edge('v', 'vm')
                    E.v(lambda v, s_=gsl: v.tensor_add(Hr[:, s_], Hr[:, s_],
                                                       hout[:, 0:32]))
                    E.v(lambda v, s_=gsl: v.tensor_add(Hi[:, s_], Hi[:, s_],
                                                       hout[:, 32:64]), mark="vm")
                    E.edge('v', 'vm')
                    E.v(lambda v, s_=gsl: v.tensor_copy(Hrb[:, s_], Hr[:, s_]))
                    E.v(lambda v, s_=gsl: v.tensor_copy(Hib[:, s_], Hi[:, s_]), mark="vm")
                    E.edge('v', 'vm')
                    E.v(lambda v, g_=g: v.tensor_tensor(Dc[:, 2:3], Zr[:, g_:g_ + 1],
                                                        Dc[:, 0:1], ALU.mult))
                    E.v(lambda v, g_=g: v.tensor_tensor(Dc[:, 3:4], Zi[:, g_:g_ + 1],
                                                        Dc[:, 1:2], ALU.mult))
                    E.v(lambda v, g_=g: v.tensor_tensor(Dc[:, 5:6], Zr[:, g_:g_ + 1],
                                                        Dc[:, 1:2], ALU.mult))
                    E.v(lambda v, g_=g: v.tensor_tensor(Dc[:, 6:7], Zi[:, g_:g_ + 1],
                                                        Dc[:, 0:1], ALU.mult), mark="vm")
                    E.edge('v', 'vm')
                    E.v(lambda v, g_=g: v.tensor_sub(Zr[:, g_:g_ + 1], Dc[:, 2:3],
                                                     Dc[:, 3:4]))
                    E.v(lambda v, g_=g: v.tensor_add(Zi[:, g_:g_ + 1], Dc[:, 5:6],
                                                     Dc[:, 6:7]), mark="vm")
                    E.edge('v', 'vm')
                E.wait('g', 'vm')
                tglob = sg * SEG + c * 64
                E.dma('g', lambda g_, t_=tglob: g_.dma_start(
                    out=w2r_d[:, t_:t_ + 64].rearrange("(g p) t -> p g t", p=128),
                    in_=w2r_b[:, :].rearrange("p (g t) -> p g t", g=2)), "gm")
                E.dma('g', lambda g_, t_=tglob: g_.dma_start(
                    out=w2i_d[:, t_:t_ + 64].rearrange("(g p) t -> p g t", p=128),
                    in_=w2i_b[:, :].rearrange("p (g t) -> p g t", g=2)), "gm")
                E.wait('v', 'tm')
                if ro == 0:
                    E.wait('v', 'gm')
                E.v(lambda v, r_=ro: v.tensor_copy(y_sb[r_:r_ + 64, :], ps_y[0:64, :]),
                    mark="vm")
                if c % 2 == 1:
                    E.wait('g', 'vm')
                    trow = sg * SEG + (c // 2) * 128
                    E.dma('g', lambda g_, r=trow: g_.dma_start(
                        out=yloc_d[r:r + 128, :], in_=y_sb[:, :]), "gm")
                E.wait('t', 'vm')
        E.wait('g', 'vm')
        E.v(lambda v: v.tensor_copy(hout[:, 0:32], Hr[:, 0:32]))
        E.v(lambda v: v.tensor_copy(hout[:, 32:64], Hi[:, 0:32]), mark="vm")
        E.wait('g', 'vm')
        E.dma('g', lambda g_: g_.dma_start(out=sfin_d[0:128, 0:32], in_=Hr[:, 0:32]), "gm")
        E.dma('g', lambda g_: g_.dma_start(out=sfin_d[0:128, 32:64], in_=Hi[:, 0:32]), "gm")
        E.dma('g', lambda g_: g_.dma_start(out=sfin_d[128:256, 0:32], in_=Hr[:, 32:64]), "gm")
        E.dma('g', lambda g_: g_.dma_start(out=sfin_d[128:256, 32:64], in_=Hi[:, 32:64]), "gm")
        E.v(lambda v: v.tensor_copy(Dc[:, 0:1], Zr[:, 0:1]))
        E.v(lambda v: v.tensor_copy(Dc[:, 1:2], Zi[:, 0:1]))
        E.v(lambda v: v.tensor_copy(Dc[:, 2:3], Zr[:, 1:2]))
        E.v(lambda v: v.tensor_copy(Dc[:, 3:4], Zi[:, 1:2]), mark="vm")
        E.wait('g', 'vm')
        E.dma('g', lambda g_: g_.dma_start(out=ztot_d[0:128, 0:2], in_=Dc[:, 0:2]), "gm")
        E.dma('g', lambda g_: g_.dma_start(out=ztot_d[128:256, 0:2], in_=Dc[:, 2:4]), "gm")
        if _os.environ.get("K_DBGC") == "1":
            E.wait('v', 'tm')
            E.v(lambda v: v.tensor_copy(y_sb[:, 0:128], ps_b[:, 0:128]))
            E.v(lambda v: v.tensor_copy(y_sb[:, 128:192], Sm_r[:, :]))
            E.v(lambda v: v.tensor_copy(y_sb[:, 192:256], Sm_i[:, :]))
            E.v(lambda v: v.tensor_copy(y_sb[:, 256:320], skrB[1][:, :]))
            E.v(lambda v: v.tensor_copy(y_sb[:, 320:352], qt_r[:, :].bitcast(F32)), mark="vm")
            E.wait('g', 'vm')
            E.dma('g', lambda g_: g_.dma_start(out=dbgc_d[:, :], in_=y_sb[:, 0:384]), "gm")
        E.g(lambda g_: g_.wait_ge(sems["gm"], E.n["gm"]))
        E.g(lambda g_: g_.wait_ge(sems["ld"], E.n["ld"]))

    def _mk(engn):
        def f(eng):
            script(Em(b, sems, active=engn, engs={engn: eng}))
        return f
    fns = {k: _mk(k) for k in ('sy', 'v', 's', 't', 'g')}
    fns['ps_x'] = ps_x
    if sb_ is not None:
        return fns
    blk = b.block()

    @blk.sync
    def _(sync):
        fns['sy'](sync)

    @blk.vector
    def _(vector):
        fns['v'](vector)

    @blk.scalar
    def _(scalar):
        fns['s'](scalar)

    @blk.tensor
    def _(tensor):
        fns['t'](tensor)

    @blk.gpsimd
    def _(gpsimd):
        fns['g'](gpsimd)

    return b.done()


def run_launch_C1(x2_full, W_in, b_in, W_qkva, b_qkva):
    nc = build_launch_C1()
    x2 = np.ascontiguousarray(np.asarray(x2_full, np.float32).reshape(NCORES * T, D))
    perm = _qkva_perm()
    mkc, mk64, bdm = _c1_consts()
    WqkT = np.ascontiguousarray(W_qkva[perm].T)
    WinT = np.ascontiguousarray(W_in.T)
    binP = np.ascontiguousarray(b_in.reshape(8, 128).T)
    bqkP = np.ascontiguousarray(b_qkva[perm].reshape(16, 128).T)
    idn = np.eye(128, dtype=ml_dtypes.bfloat16)
    in_maps = []
    for c in range(NCORES):
        xs = x2[c * T:(c + 1) * T]
        in_maps.append({"x2T": np.ascontiguousarray(xs.T), "WinT": WinT,
                        "WqkT": WqkT, "binP": binP, "bqkP": bqkP,
                        "mkc": mkc, "mk64": mk64, "bdm": bdm, "idn": idn})
    res = run_bass_kernel_spmd(nc, in_maps, core_ids=list(range(NCORES)))
    return [dict(r) for r in res.results]


def _qkva_perm():
    perm = np.zeros(2048, np.int64)
    comp_cc_ri = [(0, 0), (0, 1), (1, 0), (1, 1), (3, 0), (3, 1), (2, 0), (2, 1)]
    for comp in range(8):
        cc, ri = comp_cc_ri[comp]
        for h in range(8):
            g, hh = h // 4, h % 4
            for d in range(32):
                new = (comp * 2 + g) * 128 + hh * 32 + d
                perm[new] = h * 256 + d * 8 + cc * 2 + ri
    return perm


def _c1_consts():
    mkc = np.zeros((128, 8), np.float32)
    for hh in range(4):
        mkc[hh * 32:(hh + 1) * 32, hh] = 1.0
        mkc[hh * 32:(hh + 1) * 32, 4 + hh] = -1.0
    mk64 = np.tril(np.ones((64, 64), np.uint8)).T.copy()
    bdm = np.zeros((128, 128), np.float32)
    for a_ in range(4):
        bdm[a_ * 32:(a_ + 1) * 32, a_ * 32:(a_ + 1) * 32] = 1.0
    return mkc, np.ascontiguousarray(mk64), bdm


def build_launch_C2(sb_=None, xch=False):
    b = sb_ or Builder()
    yloc_d = b.inp("yloc", [T, 512], F32)
    w2r_d = b.inp("w2r", [256, T], F32)
    w2i_d = b.inp("w2i", [256, T], F32)
    x2_d = b.inp("x2", [T, D], F32)
    wyT_d = b.inp("wyT", [512, D], F32)
    woT_d = b.inp("woT", [D, D], F32)
    byP_d = b.inp("byP", [128, 8], F32)
    boP_d = b.inp("boP", [128, 8], F32)
    h0r_d = b.inp("h0r", [256, 32], F32)
    h0i_d = b.inp("h0i", [256, 32], F32)
    if xch:
        sfin_d = b.inp("Sfin", [256, 64], F32)
        ztot_d = b.inp("Ztot", [256, 2], F32)
        gsf_d = b.inp("gsf", [2048, 64], F32)
        gzt_d = b.inp("gzt", [2048, 2], F32)
        selP_d = b.inp("selP", [128, 9], F32)
        x3T_d = b.inp("x3T", [D, T], F32)
    else:
        spr_d = b.inp("spr", [256, 32], F32)
        spi_d = b.inp("spi", [256, 32], F32)
        zp_d = b.inp("zp", [256, 2], F32)
    idn_d = b.inp("idn", [128, 128], BF16)
    x3_d = b.out("x3", [T, D], F32)

    wy_b = b.sb("wy_b", [128, 4 * D], BF16)
    wo_b = b.sb("wo_b", [128, 8 * D], BF16)
    wstg = b.sb("wstg", [128, D], F32)
    hstg = b.sb("hstg", [128, 256], F32)
    zst = b.sb("zst", [128, 4], F32)
    hta = b.sb("hta", [128, 32], F32)
    htb = b.sb("htb", [128, 32], F32)
    hinr = b.sb("hinr", [128, 64], F32)
    hini = b.sb("hini", [128, 64], F32)
    Hbr = [b.sb(f"Hbr{g}", [128, 128], BF16) for g in range(2)]
    Hbi = [b.sb(f"Hbi{g}", [128, 128], BF16) for g in range(2)]
    Hbn = [b.sb(f"Hbn{g}", [128, 128], BF16) for g in range(2)]
    w2st = [b.sb(f"w2st{i}", [128, 512], F32) for i in range(2)]
    w2b = [b.sb(f"w2b{i}", [128, 512], BF16) for i in range(2)]
    ylb = [b.sb(f"ylb{i}", [128, 512], F32) for i in range(2)]
    x2b = [b.sb(f"x2b{i}", [128, D], F32) for i in range(2)]
    ytb = b.sb("ytb", [128, 512], BF16)
    yTt = b.sb("yTt", [128, 512], BF16)
    y2b = b.sb("y2b", [128, D], BF16)
    attb = b.sb("attb", [128, D], BF16)
    att_f = b.sb("att_f", [128, D], F32)
    x3_s = [b.sb(f"x3_s{i}", [128, D], F32) for i in range(2)]
    byP = b.sb("byP_s", [128, 8], F32)
    boP = b.sb("boP_s", [128, 8], F32)
    lnc = b.sb("lnc", [128, 8], F32)
    cpool = b.sb("cpool", [128, 8], F32)
    idn = b.sb("idn_s", [128, 128], BF16)
    if xch:
        gsf_s = b.sb("gsf_s", [128, 1024], F32)
        gzt_s = b.sb("gzt_s", [128, 32], F32)
        selS = b.sb("selS", [128, 9], F32)
        xTo = [b.sb(f"xTo{i}", [128, D], F32) for i in range(2)]

    ps_c = b.ps("ps_c", [128, 512])
    ps_y2 = b.ps("ps_y2", [128, 1024])
    ps_at = b.ps("ps_at", [128, 1024])
    ps_t = b.ps("ps_t", [128, 1024], BF16)
    ps_x = b.ps("ps_x", [128, 128])

    semnames = ["ld", "wcv", "tm", "sm", "vm", "gm", "yev", "ate", "rv",
                "cc", "xev"]
    sems = {k: b.sem(k) for k in semnames}

    def script(E):
        def tgroup(mms, mark="tm"):
            for f in mms:
                E.t(f)
            E.t(lambda t: t.matmul(ps_x[:, :], idn[:, :], idn[:, :],
                                   start=True, stop=True), mark=mark)

        E.v(lambda v: v.memset(cpool[:, 0:1], LN_EPS), mark="vm")
        for dst, srcd in ((byP, byP_d), (boP, boP_d), (idn, idn_d)):
            E.dma('sy', lambda sy, d=dst, s=srcd: sy.dma_start(out=d[:, :], in_=s[:, :]), "ld")
        if not xch:
            E.dma('sy', lambda sy: sy.dma_start(out=zst[:, 0:2], in_=zp_d[0:128, :]), "ld")
            E.dma('sy', lambda sy: sy.dma_start(out=zst[:, 2:4], in_=zp_d[128:256, :]), "ld")
        srcs = (h0r_d, h0i_d) if xch else (h0r_d, h0i_d, spr_d, spi_d)
        for gi, srcd in enumerate(srcs):
            for g in range(2):
                E.dma('sy', lambda sy, s=srcd, g_=g, o=gi * 64:
                      sy.dma_start(out=hstg[:, o + g_ * 32:o + g_ * 32 + 32],
                                   in_=s[g_ * 128:(g_ + 1) * 128, :]), "ld")
        if xch:
            E.dma('sy', lambda sy: sy.dma_start(out=selS[:, :], in_=selP_d[:, :]), "ld")
            E.g(lambda g_: g_.collective_compute(
                "AllGather", ALU.bypass,
                replica_groups=[[0, 1, 2, 3, 4, 5, 6, 7]],
                ins=[sfin_d[:, :]], outs=[gsf_d[:, :]]), mark="cc")
            E.g(lambda g_: g_.collective_compute(
                "AllGather", ALU.bypass,
                replica_groups=[[0, 1, 2, 3, 4, 5, 6, 7]],
                ins=[ztot_d[:, :]], outs=[gzt_d[:, :]]), mark="cc")
            E.wait('g', 'cc')
            for c in range(8):
                for g in range(2):
                    ti = 2 * c + g
                    E.dma('g', lambda gp, c_=c, g_=g, t_=ti: gp.dma_start(
                        out=gsf_s[:, t_ * 64:(t_ + 1) * 64],
                        in_=gsf_d[c_ * 256 + g_ * 128:c_ * 256 + g_ * 128 + 128, :]),
                        "gm")
                    E.dma('g', lambda gp, c_=c, g_=g, t_=ti: gp.dma_start(
                        out=gzt_s[:, t_ * 2:(t_ + 1) * 2],
                        in_=gzt_d[c_ * 256 + g_ * 128:c_ * 256 + g_ * 128 + 128, :]),
                        "gm")
        for wi in range(12):
            E.wait('sy', 'wcv', wi)
            srcd = wyT_d[wi * 128:(wi + 1) * 128, :] if wi < 4 else \
                woT_d[(wi - 4) * 128:(wi - 4 + 1) * 128, :]
            E.dma('sy', lambda sy, s=srcd: sy.dma_start(out=wstg[:, :], in_=s), "ld")
            E.wait('v', 'ld')
            dst = wy_b[:, wi * D:(wi + 1) * D] if wi < 4 else \
                wo_b[:, (wi - 4) * D:(wi - 4 + 1) * D]
            E.v(lambda v, d=dst: v.tensor_copy(d, wstg[:, :]), mark="wcv")

        E.wait('v', 'ld')
        if xch:
            E.wait('v', 'gm')
            E.v(lambda v: v.memset(zst[:, 0:4], 0.0))
            E.v(lambda v: v.memset(hstg[:, 128:256], 0.0), mark="vm")
            E.edge('v', 'vm')
            for g in range(2):
                for c in range(8):
                    ti = 2 * c + g
                    sc = selS[:, c:c + 1]
                    E.v(lambda v, t_=ti, s_=sc: v.tensor_scalar(
                        hta[:, :], gsf_s[:, t_ * 64:t_ * 64 + 32], s_, None,
                        ALU.mult))
                    E.v(lambda v, t_=ti, s_=sc: v.tensor_scalar(
                        htb[:, :], gsf_s[:, t_ * 64 + 32:t_ * 64 + 64], s_, None,
                        ALU.mult), mark="vm")
                    E.edge('v', 'vm')
                    E.v(lambda v, g_=g: v.tensor_add(
                        hstg[:, 128 + g_ * 32:128 + g_ * 32 + 32],
                        hstg[:, 128 + g_ * 32:128 + g_ * 32 + 32], hta[:, :]))
                    E.v(lambda v, g_=g: v.tensor_add(
                        hstg[:, 192 + g_ * 32:192 + g_ * 32 + 32],
                        hstg[:, 192 + g_ * 32:192 + g_ * 32 + 32], htb[:, :]),
                        mark="vm")
                    E.edge('v', 'vm')
                    E.v(lambda v, t_=ti, s_=sc: v.tensor_scalar(
                        hta[:, 0:1], gzt_s[:, t_ * 2:t_ * 2 + 1], s_, None,
                        ALU.mult))
                    E.v(lambda v, t_=ti, s_=sc: v.tensor_scalar(
                        htb[:, 0:1], gzt_s[:, t_ * 2 + 1:t_ * 2 + 2], s_, None,
                        ALU.mult), mark="vm")
                    E.edge('v', 'vm')
                    E.v(lambda v, g_=g: v.tensor_add(
                        zst[:, 2 * g_:2 * g_ + 1], zst[:, 2 * g_:2 * g_ + 1],
                        hta[:, 0:1]))
                    E.v(lambda v, g_=g: v.tensor_add(
                        zst[:, 2 * g_ + 1:2 * g_ + 2],
                        zst[:, 2 * g_ + 1:2 * g_ + 2], htb[:, 0:1]), mark="vm")
                    E.edge('v', 'vm')
                E.v(lambda v, g_=g: v.tensor_add(
                    zst[:, 2 * g_:2 * g_ + 1], zst[:, 2 * g_:2 * g_ + 1],
                    selS[:, 8:9]), mark="vm")
                E.edge('v', 'vm')
        for g in range(2):
            zr = zst[:, 2 * g:2 * g + 1]
            zi = zst[:, 2 * g + 1:2 * g + 2]
            h0r_g = hstg[:, g * 32:g * 32 + 32]
            h0i_g = hstg[:, 64 + g * 32:64 + g * 32 + 32]
            spr_g = hstg[:, 128 + g * 32:128 + g * 32 + 32]
            spi_g = hstg[:, 192 + g * 32:192 + g * 32 + 32]
            gs = slice(g * 32, g * 32 + 32)
            E.v(lambda v, h=h0r_g, z=zr: v.tensor_scalar(hta[:, :], h, z, None, ALU.mult))
            E.v(lambda v, h=h0i_g, z=zi: v.tensor_scalar(htb[:, :], h, z, None, ALU.mult),
                mark="vm")
            E.edge('v', 'vm')
            E.v(lambda v, s_=gs: v.tensor_sub(hinr[:, s_], hta[:, :], htb[:, :]))
            E.v(lambda v, h=h0i_g, z=zr: v.tensor_scalar(hta[:, :], h, z, None, ALU.mult))
            E.v(lambda v, h=h0r_g, z=zi: v.tensor_scalar(htb[:, :], h, z, None, ALU.mult),
                mark="vm")
            E.edge('v', 'vm')
            E.v(lambda v, s_=gs: v.tensor_add(hini[:, s_], hta[:, :], htb[:, :]),
                mark="vm")
            E.edge('v', 'vm')
            E.v(lambda v, s_=gs, sp=spr_g: v.tensor_add(hinr[:, s_], hinr[:, s_], sp))
            E.v(lambda v, s_=gs, sp=spi_g: v.tensor_add(hini[:, s_], hini[:, s_], sp),
                mark="vm")
            E.edge('v', 'vm')
        for g in range(2):
            E.v(lambda v, g_=g: v.memset(Hbr[g_][:, :], 0.0))
            E.v(lambda v, g_=g: v.memset(Hbi[g_][:, :], 0.0), mark="vm")
            E.edge('v', 'vm')
            for hh in range(4):
                hs = slice(hh * 32, hh * 32 + 32)
                E.v(lambda v, g_=g, s_=hs: v.tensor_copy(
                    Hbr[g_][s_, s_], hinr[s_, g_ * 32:g_ * 32 + 32]))
                E.v(lambda v, g_=g, s_=hs: v.tensor_copy(
                    Hbi[g_][s_, s_], hini[s_, g_ * 32:g_ * 32 + 32]), mark="vm")
            E.edge('v', 'vm')
            E.v(lambda v, g_=g: v.tensor_scalar_mul(Hbn[g_][:, :], Hbi[g_][:, :], -1.0),
                mark="vm")

        for tc in range(NT):
            par = tc % 2
            E.wait('sy', 'wcv', 12 + max(0, 4 * (tc - 1)))
            for pi, srcd in enumerate((w2r_d, w2i_d)):
                for g in range(2):
                    E.dma('sy', lambda sy, s=srcd, g_=g, p=par, o=pi * 256:
                          sy.dma_start(out=w2st[p][:, o + g_ * 128:o + g_ * 128 + 128],
                                       in_=s[g_ * 128:(g_ + 1) * 128,
                                             tc * 128:(tc + 1) * 128]), "ld")
            E.wait('sy', 'yev', max(0, tc - 1))
            E.dma('sy', lambda sy, p=par, t_=tc: sy.dma_start(
                out=ylb[p][:, :], in_=yloc_d[t_ * 128:(t_ + 1) * 128, :]), "ld")
            E.wait('sy', 'rv', max(0, tc - 1))
            E.dma('sy', lambda sy, p=par, t_=tc: sy.dma_start(
                out=x2b[p][:, :], in_=x2_d[t_ * 128:(t_ + 1) * 128, :]), "ld")
            E.wait('v', 'ld')
            for q in range(4):
                E.v(lambda v, p=par, q_=q: v.tensor_copy(
                    w2b[p][:, q_ * 128:(q_ + 1) * 128],
                    w2st[p][:, q_ * 128:(q_ + 1) * 128]), mark="wcv")
            E.wait('t', 'wcv')
            E.wait('t', 'yev')
            if tc == 0:
                E.wait('t', 'vm')
            mm = []
            for g in range(2):
                for comp in range(2):
                    co = comp * 256 + g * 128
                    wr = w2b[par][:, g * 128:(g + 1) * 128]
                    wi_ = w2b[par][:, 256 + g * 128:256 + (g + 1) * 128]
                    if comp == 0:
                        ops = [(wr, Hbr[g]), (wi_, Hbn[g])]
                    else:
                        ops = [(wr, Hbi[g]), (wi_, Hbr[g])]
                    for oi, (lh, rh) in enumerate(ops):
                        mm.append(lambda t, lh_=lh, rh_=rh, oi_=oi, co_=co:
                                  t.matmul(ps_c[:, co_:co_ + 128], lh_[:, :], rh_[:, :],
                                           start=(oi_ == 0), stop=(oi_ == 1)))
            tgroup(mm)
            E.wait('v', 'tm')
            E.v(lambda v, p=par: v.tensor_tensor(ytb[:, :], ps_c[:, :], ylb[p][:, :],
                                                 ALU.add), mark="yev")
            E.wait('t', 'yev')
            tgroup([(lambda t, f=ft: t.transpose(ps_t[:, f * 128:(f + 1) * 128],
                                                 ytb[:, f * 128:(f + 1) * 128],
                                                 idn[:, :])) for ft in range(4)])
            E.wait('v', 'tm')
            E.v(lambda v: v.tensor_copy(yTt[:, :], ps_t[:, 0:512]), mark="vm")
            E.wait('t', 'vm')
            E.wait('t', 'sm')
            mm = []
            for jt in range(8):
                for ft in range(4):
                    mm.append(lambda t, j=jt, f=ft: t.matmul(
                        ps_y2[:, j * 128:(j + 1) * 128],
                        wy_b[:, f * D + j * 128:f * D + j * 128 + 128],
                        yTt[:, f * 128:(f + 1) * 128],
                        start=(f == 0), stop=(f == 3)))
            tgroup(mm)
            E.wait('s', 'tm')
            for jt in range(8):
                E.s(lambda s, j=jt: s.activation(y2b[:, j * 128:(j + 1) * 128],
                                                 ps_y2[:, j * 128:(j + 1) * 128],
                                                 AF.Silu, bias=byP[:, j:j + 1]),
                    mark="sm")
            E.wait('t', 'sm')
            E.wait('t', 'ate')
            mm = []
            for ot in range(8):
                for jt in range(8):
                    mm.append(lambda t, o=ot, j=jt: t.matmul(
                        ps_at[:, o * 128:(o + 1) * 128],
                        wo_b[:, j * D + o * 128:j * D + o * 128 + 128],
                        y2b[:, j * 128:(j + 1) * 128],
                        start=(j == 0), stop=(j == 7)))
            tgroup(mm)
            E.wait('v', 'tm')
            for ot in range(8):
                E.v(lambda v, o=ot: v.tensor_scalar(attb[:, o * 128:(o + 1) * 128],
                                                    ps_at[:, o * 128:(o + 1) * 128],
                                                    boP[:, o:o + 1], None, ALU.add),
                    mark=("ate" if ot == 7 else None))
            E.wait('t', 'ate')
            E.wait('t', 'rv')
            tgroup([(lambda t, o=ot: t.transpose(ps_t[:, o * 128:(o + 1) * 128],
                                                 attb[:, o * 128:(o + 1) * 128],
                                                 idn[:, :])) for ot in range(8)])
            E.wait('v', 'tm')
            E.v(lambda v: v.tensor_copy(att_f[:, :], ps_t[:, :]))
            E.v(lambda v, p=par: v.tensor_add(att_f[:, :], att_f[:, :], x2b[p][:, :]),
                mark="rv")
            E.v(lambda v: v.tensor_reduce(lnc[:, 0:1], att_f[:, :], AX.X, ALU.add),
                mark="vm")
            E.wait('s', 'vm')
            E.s(lambda s: s.activation(wstg[:, :], att_f[:, :], AF.Square,
                                       accum_out=lnc[:, 1:2]), mark="sm")
            E.wait('v', 'sm')
            E.v(lambda v: v.tensor_scalar(lnc[:, 2:3], lnc[:, 0:1], 1.0 / D, None,
                                          ALU.mult))
            E.v(lambda v: v.tensor_scalar(lnc[:, 3:4], lnc[:, 1:2], 1.0 / D, None,
                                          ALU.mult), mark="vm")
            E.edge('v', 'vm')
            E.v(lambda v: v.tensor_tensor(lnc[:, 4:5], lnc[:, 2:3], lnc[:, 2:3],
                                          ALU.mult), mark="vm")
            E.edge('v', 'vm')
            E.v(lambda v: v.tensor_sub(lnc[:, 5:6], lnc[:, 3:4], lnc[:, 4:5]),
                mark="vm")
            E.wait('s', 'vm')
            E.s(lambda s: s.activation(lnc[:, 6:7], lnc[:, 5:6], AF.Sqrt,
                                       bias=cpool[:, 0:1]), mark="sm")
            E.wait('v', 'sm')
            E.v(lambda v: v.reciprocal(lnc[:, 7:8], lnc[:, 6:7]), mark="vm")
            E.edge('v', 'vm')
            E.wait('v', 'gm', max(0, E.n['gm'] - 16))
            E.v(lambda v, p=par: v.tensor_scalar(x3_s[p][:, :], att_f[:, :],
                                                 lnc[:, 2:3], lnc[:, 7:8],
                                                 ALU.subtract, ALU.mult), mark="vm")
            E.wait('g', 'vm')
            E.dma('g', lambda g_, p=par, t_=tc: g_.dma_start(
                out=x3_d[t_ * 128:(t_ + 1) * 128, :], in_=x3_s[p][:, :]), "gm")
            if xch:
                E.v(lambda v, p=par: v.tensor_copy(attb[:, :], x3_s[p][:, :]),
                    mark="vm")
                E.wait('t', 'vm')
                E.wait('t', 'rv')
                E.wait('t', 'xev')
                tgroup([(lambda t, o=ot: t.transpose(
                    ps_t[:, o * 128:(o + 1) * 128], attb[:, o * 128:(o + 1) * 128],
                    idn[:, :])) for ot in range(8)])
                E.wait('v', 'tm')
                E.wait('v', 'gm', max(0, E.n['gm'] - 144))
                E.v(lambda v, p=par: v.tensor_copy(xTo[p][:, :], ps_t[:, :]),
                    mark="xev")
                E.wait('g', 'xev')
                for ot in range(8):
                    E.dma('g', lambda gp, p=par, o=ot, t_=tc: gp.dma_start(
                        out=x3T_d[o * 128:(o + 1) * 128, t_ * 128:(t_ + 1) * 128],
                        in_=xTo[p][:, o * 128:(o + 1) * 128]), "gm")
        E.g(lambda g_: g_.wait_ge(sems["gm"], E.n["gm"]))
        E.g(lambda g_: g_.wait_ge(sems["ld"], E.n["ld"]))

    def _mk(engn):
        def f(eng):
            script(Em(b, sems, active=engn, engs={engn: eng}))
        return f
    fns = {k: _mk(k) for k in ('sy', 'v', 's', 't', 'g')}
    fns['ps_x'] = ps_x
    if sb_ is not None:
        return fns
    blk = b.block()

    @blk.sync
    def _(sync):
        fns['sy'](sync)

    @blk.vector
    def _(vector):
        fns['v'](vector)

    @blk.scalar
    def _(scalar):
        fns['s'](scalar)

    @blk.tensor
    def _(tensor):
        fns['t'](tensor)

    @blk.gpsimd
    def _(gpsimd):
        fns['g'](gpsimd)

    return b.done()


def _perm_y():
    our = np.arange(512)
    comp = our // 256
    rem = our % 256
    return (rem // 32) * 64 + (rem % 32) * 2 + comp


def run_launch_C2(x2_full, c1res, W_y, b_y, W_out, b_out, h0_re, h0_im):
    nc = build_launch_C2()
    x2 = np.ascontiguousarray(np.asarray(x2_full, np.float32).reshape(NCORES * T, D))
    wyT = np.ascontiguousarray(W_y.T[_perm_y()])
    woT = np.ascontiguousarray(W_out.T)
    byP = np.ascontiguousarray(b_y.reshape(8, 128).T)
    boP = np.ascontiguousarray(b_out.reshape(8, 128).T)
    h0r = np.ascontiguousarray(h0_re.reshape(256, 32).astype(np.float32))
    h0i = np.ascontiguousarray(h0_im.reshape(256, 32).astype(np.float32))
    idn = np.eye(128, dtype=ml_dtypes.bfloat16)
    in_maps = []
    for c in range(NCORES):
        bb, hf = c // 2, c % 2
        if hf == 0:
            spr = np.zeros((256, 32), np.float32)
            spi = np.zeros((256, 32), np.float32)
            zp = np.zeros((256, 2), np.float32); zp[:, 0] = 1.0
        else:
            psf = np.asarray(c1res[2 * bb]["Sfin"])
            zp = np.ascontiguousarray(np.asarray(c1res[2 * bb]["Ztot"]))
            spr = np.ascontiguousarray(psf[:, 0:32])
            spi = np.ascontiguousarray(psf[:, 32:64])
        in_maps.append({"yloc": np.ascontiguousarray(np.asarray(c1res[c]["yloc"])),
                        "w2r": np.ascontiguousarray(np.asarray(c1res[c]["W2r"])),
                        "w2i": np.ascontiguousarray(np.asarray(c1res[c]["W2i"])),
                        "x2": np.ascontiguousarray(x2[c * T:(c + 1) * T]),
                        "wyT": wyT, "woT": woT, "byP": byP, "boP": boP,
                        "h0r": h0r, "h0i": h0i, "spr": spr, "spi": spi, "zp": zp,
                        "idn": idn})
    res = run_bass_kernel_spmd(nc, in_maps, core_ids=list(range(NCORES)))
    x3 = np.concatenate([np.asarray(r["x3"]) for r in res.results], 0)
    return x3.reshape(4, 8192, D)


def make_stage_T(b, src_name, dst_name):
    src_d = b.io[src_name]
    dst_d = b.io[dst_name]
    idn_d = b.io["idn"]
    xstg = [b.sb(f"xstg{i}", [128, D], F32) for i in range(2)]
    xb = [b.sb(f"xb{i}", [128, D], BF16) for i in range(2)]
    xo = [b.sb(f"xo{i}", [128, D], F32) for i in range(2)]
    idn = b.sb("idn_s", [128, 128], BF16)
    ps_t = b.ps("ps_t", [128, 1024], BF16)
    ps_x = b.ps("ps_x", [128, 128])
    sems = {k: b.sem(k) for k in ("ld", "cv", "tm", "ev", "gm")}

    def script(E):
        E.dma('sy', lambda sy: sy.dma_start(out=idn[:, :], in_=idn_d[:, :]), "ld")
        for tc in range(NT):
            par = tc % 2
            E.wait('sy', 'cv', max(0, tc - 1))
            E.dma('sy', lambda sy, p=par, t_=tc: sy.dma_start(
                out=xstg[p][:, :], in_=src_d[t_ * 128:(t_ + 1) * 128, :]), "ld")
            E.wait('v', 'ld')
            E.v(lambda v, p=par: v.tensor_copy(xb[p][:, :], xstg[p][:, :]), mark="cv")
            E.wait('t', 'cv')
            E.wait('t', 'ev')
            for ot in range(8):
                E.t(lambda t, p=par, o=ot: t.transpose(
                    ps_t[:, o * 128:(o + 1) * 128], xb[p][:, o * 128:(o + 1) * 128],
                    idn[:, :]))
            E.t(lambda t: t.matmul(ps_x[:, :], idn[:, :], idn[:, :],
                                   start=True, stop=True), mark="tm")
            E.wait('v', 'tm')
            E.wait('v', 'gm', max(0, E.n['gm'] - 128))
            E.v(lambda v, p=par: v.tensor_copy(xo[p][:, :], ps_t[:, :]), mark="ev")
            E.wait('g', 'ev')
            for ot in range(8):
                E.dma('g', lambda gp, p=par, o=ot, t_=tc: gp.dma_start(
                    out=dst_d[o * 128:(o + 1) * 128, t_ * 128:(t_ + 1) * 128],
                    in_=xo[p][:, o * 128:(o + 1) * 128]), "gm")
        E.g(lambda g_: g_.wait_ge(sems["gm"], E.n["gm"]))
        E.g(lambda g_: g_.wait_ge(sems["ld"], E.n["ld"]))

    def _mk(engn):
        def f(eng):
            script(Em(b, sems, active=engn, engs={engn: eng}))
        return f
    fns = {k: _mk(k) for k in ('sy', 'v', 's', 't', 'g')}
    fns['ps_x'] = ps_x
    return fns


BAR_PER = 20


def build_fused():
    rb = Builder(num_devices=8)
    io = {}
    io["xT"] = rb.inp("xT", [D, 2 * T], F32)
    io["xo"] = rb.inp("xo", [T, D], F32)
    io["WqT"] = rb.inp("WqT", [D, D], F32)
    io["WkT"] = rb.inp("WkT", [D, D], F32)
    io["WvT"] = rb.inp("WvT", [D, D], F32)
    io["hf"] = rb.inp("hf", [1, 1], F32)
    io["wg"] = rb.inp("wg", [2816, D], F32)
    io["wu"] = rb.inp("wu", [2816, D], F32)
    io["wd"] = rb.inp("wd", [D, 2816], F32)
    io["idn"] = rb.inp("idn", [128, 128], BF16)
    io["WinT"] = rb.inp("WinT", [D, D], F32)
    io["WqkT"] = rb.inp("WqkT", [D, 2048], F32)
    io["binP"] = rb.inp("binP", [128, 8], F32)
    io["bqkP"] = rb.inp("bqkP", [128, 16], F32)
    io["mkc"] = rb.inp("mkc", [128, 8], F32)
    io["mk64"] = rb.inp("mk64", [64, 64], U8)
    io["bdm"] = rb.inp("bdm", [128, 128], F32)
    io["wyT"] = rb.inp("wyT", [512, D], F32)
    io["woT"] = rb.inp("woT", [D, D], F32)
    io["byP"] = rb.inp("byP", [128, 8], F32)
    io["boP"] = rb.inp("boP", [128, 8], F32)
    io["h0r"] = rb.inp("h0r", [256, 32], F32)
    io["h0i"] = rb.inp("h0i", [256, 32], F32)
    io["selP"] = rb.inp("selP", [128, 9], F32)
    io["wgT"] = rb.inp("wgT", [D, 3072], I32)
    io["wuT"] = rb.inp("wuT", [D, 3072], I32)
    io["wdT"] = rb.inp("wdT", [3072, D], I32)
    io["x4"] = rb.out("x4", [T, D], F32)

    import os as _os
    _dbg = _os.environ.get("K_FDBG")
    mk = rb.out if _dbg else rb.scratch
    io["x1"] = mk("x1_h", [T, D], F32)
    io["x2"] = mk("x2_h", [T, D], F32)
    io["x2T"] = rb.scratch("x2T_h", [D, T], F32)
    io["yloc"] = mk("yloc_h", [T, 512], F32)
    w2r = rb.scratch("w2r_h", [256, T], F32)
    w2i = rb.scratch("w2i_h", [256, T], F32)
    io["W2r"] = w2r; io["w2r"] = w2r
    io["W2i"] = w2i; io["w2i"] = w2i
    io["Sfin"] = rb.scratch("sfin_h", [256, 64], F32)
    io["Ztot"] = rb.scratch("ztot_h", [256, 2], F32)
    io["gsf"] = rb.scratch("gsf_h", [2048, 64], F32)
    io["gzt"] = rb.scratch("gzt_h", [2048, 2], F32)
    io["x3"] = mk("x3_h", [T, D], F32)
    io["x3T"] = rb.scratch("x3T_h", [D, T], F32)

    barv = rb.sb("barv", [128, 8], F32)
    idn_r = rb.sb("idn_r", [128, 128], BF16)
    bars = rb.sem("bars")

    stage_defs = [
        ("A_", lambda sb: build_launch_A(sb_=sb)),
        ("B_", lambda sb: build_launch_B(sb_=sb)),
        ("T1_", lambda sb: make_stage_T(sb, "x2", "x2T")),
        ("C1_", lambda sb: build_launch_C1(sb_=sb)),
        ("C2_", lambda sb: build_launch_C2(sb_=sb, xch=True)),
        ("D_", lambda sb: build_launch_D(sb_=sb)),
    ]
    stages = []
    for pfx, fn in stage_defs:
        sb = StageBuilder(rb, io, pfx)
        stages.append(fn(sb))
        sb.stack.close()

    def bar_emit(engn, eng, idx, ps_x):
        tgt = 32 + BAR_PER * idx
        if engn == 'v':
            eng.memset(barv[:, 0:1], 0.0).then_inc(bars, 1)
        elif engn == 's':
            if idx == 1:
                eng.wait_ge(bars, 32)
            eng.activation(barv[0:1, 1:2], barv[0:1, 3:4],
                           AF.Abs).then_inc(bars, 1)
        elif engn == 't':
            if idx == 1:
                eng.wait_ge(bars, 32)
            eng.matmul(ps_x[:, :], idn_r[:, :], idn_r[:, :],
                       start=True, stop=True).then_inc(bars, 1)
        elif engn == 'g':
            eng.memset(barv[:, 2:3], 0.0).then_inc(bars, 1)
        elif engn == 'sy':
            eng.dma_start(out=barv[0:1, 4:5],
                          in_=io["hf"][0:1, 0:1]).then_inc(bars, 16)
        eng.wait_ge(bars, tgt)

    def section(engn):
        def f(eng):
            if engn == 'sy':
                eng.dma_start(out=idn_r[:, :], in_=io["idn"][:, :]).then_inc(bars, 16)
                eng.dma_start(out=barv[0:1, 3:4],
                              in_=io["hf"][0:1, 0:1]).then_inc(bars, 16)
            for si, st in enumerate(stages):
                st[engn](eng)
                bar_emit(engn, eng, si + 1, st['ps_x'])
        return f

    blk = rb.block()

    @blk.sync
    def _(sync):
        section('sy')(sync)

    @blk.vector
    def _(vector):
        section('v')(vector)

    @blk.scalar
    def _(scalar):
        section('s')(scalar)

    @blk.tensor
    def _(tensor):
        section('t')(tensor)

    @blk.gpsimd
    def _(gpsimd):
        section('g')(gpsimd)

    return rb.done()


def kernel_fused(inputs):
    nc = build_fused()
    perm = _qkva_perm()
    mkc, mk64, bdm = _c1_consts()
    idn = np.eye(128, dtype=ml_dtypes.bfloat16)
    common = {
        "WqT": np.ascontiguousarray(inputs["L0_attn_Wq"].T),
        "WkT": np.ascontiguousarray(inputs["L0_attn_Wk"].T),
        "WvT": np.ascontiguousarray(inputs["L0_attn_Wv"].T),
        "hf": np.zeros((1, 1), np.float32),
        "wg": inputs["L0_ffn_w_gate"], "wu": inputs["L0_ffn_w_up"],
        "wd": inputs["L0_ffn_w_down"], "idn": idn,
        "WinT": np.ascontiguousarray(inputs["L1_attn_W_in"].T),
        "WqkT": np.ascontiguousarray(inputs["L1_attn_W_qkva"][perm].T),
        "binP": np.ascontiguousarray(inputs["L1_attn_b_in"].reshape(8, 128).T),
        "bqkP": np.ascontiguousarray(
            inputs["L1_attn_b_qkva"][perm].reshape(16, 128).T),
        "mkc": mkc, "mk64": mk64, "bdm": bdm,
        "wyT": np.ascontiguousarray(inputs["L1_attn_W_y"].T[_perm_y()]),
        "woT": np.ascontiguousarray(inputs["L1_attn_W_out"].T),
        "byP": np.ascontiguousarray(inputs["L1_attn_b_y"].reshape(8, 128).T),
        "boP": np.ascontiguousarray(inputs["L1_attn_b_out"].reshape(8, 128).T),
        "h0r": np.ascontiguousarray(
            inputs["L1_attn_h0_re"].reshape(256, 32).astype(np.float32)),
        "h0i": np.ascontiguousarray(
            inputs["L1_attn_h0_im"].reshape(256, 32).astype(np.float32)),
        "wgT": np.ascontiguousarray(inputs["L1_ffn_w_gate"].T),
        "wuT": np.ascontiguousarray(inputs["L1_ffn_w_up"].T),
        "wdT": np.ascontiguousarray(inputs["L1_ffn_w_down"].T),
    }
    x = inputs["L0_attn_x"]
    in_maps = []
    for c in range(NCORES):
        bb, hfv = c // 2, c % 2
        xb = x[bb]
        own = xb[hfv * T:(hfv + 1) * T]
        other = xb[(1 - hfv) * T:(2 - hfv) * T]
        xperm = np.concatenate([own, other], 0)
        selP = np.zeros((128, 9), np.float32)
        if hfv == 1:
            selP[:, 2 * bb] = 1.0
        else:
            selP[:, 8] = 1.0
        in_maps.append({**common,
                        "xT": np.ascontiguousarray(xperm.T),
                        "xo": np.ascontiguousarray(own),
                        "selP": selP})
    res = run_bass_kernel_spmd(nc, in_maps, core_ids=list(range(NCORES)),
                               has_collectives=True)
    x4 = np.concatenate([np.asarray(r["x4"]) for r in res.results], 0)
    return np.ascontiguousarray(x4.reshape(4, 8192, D).astype(np.float32))
